# version 01
# baseline (speedup 1.0000x reference)
import numpy as np
import ml_dtypes
from contextlib import ExitStack

import concourse.bass as bass
import concourse.mybir as mybir
from concourse.bass_utils import run_bass_kernel_spmd

F32 = mybir.dt.float32
BF16 = mybir.dt.bfloat16
AF = mybir.ActivationFunctionType
ALU = mybir.AluOpType
AX = mybir.AxisListType
NBF = ml_dtypes.bfloat16

NTILES = 16
NDC = 8
E = 8
LN_EPS = 1e-5
DEN_EPS = 1e-6

QPAIRS = [(0, 4), (1, 5), (2, 6), (3, 7), (8, 12), (9, 13), (10, 14), (11, 15)]
QPERM = np.array([h * 64 + d for pair in QPAIRS for h in pair for d in range(64)])


def q_slot(hq):
    fcq = hq % 4 + 4 * (hq // 8)
    base = 64 * ((hq // 4) % 2)
    assert QPAIRS[fcq][base // 64] == hq
    return fcq, base


class Ctr:
    def __init__(self, h):
        self.h = h
        self.v = 0

    def tick(self, inst, k=1):
        inst.then_inc(self.h, k)
        self.v += k
        return self.v

    def mark(self, eng, k=1):
        eng.sem_inc(self.h, k)
        self.v += k
        return self.v

    def dtick(self, eng, k=1):
        eng.drain().then_inc(self.h, k)
        self.v += k
        return self.v


def build_nc(stages=("ffn1", "attn", "ffn2")):
    import os
    trunc = os.environ.get("KTRUNC", "")
    nc = bass.Bass("TRN2", target_bir_lowering=False)

    def din(name, shape, dtype=F32):
        return nc.dram_tensor(name, shape, dtype, kind="ExternalInput").ap()

    x_h = din("x", [2048, 1024])
    wq_h = din("wq", [1024, 1024])
    bq_h = din("bq", [1, 1024])
    wkT_h = din("wkT", [256, 1024])
    wvT_h = din("wvT", [256, 1024])
    wcT_h = din("wcT", [64, 256])
    wee_h = din("wee", [64, 256])
    bkc_h = din("bkc", [128, 2])
    bvc_h = din("bvc", [128, 2])
    wo_h = din("wo", [1024, 1024])
    bo_h = din("bo", [1, 1024])
    gam_h = din("gam", [1, 1024])
    bet_h = din("bet", [1, 1024])
    rwT_h = din("rwT", [1024, 8])
    rb_h = din("rb", [1, 8])
    ew_h = din("ew", [8, 1024, 1024])
    eb_h = din("eb", [8, 1024])
    ident_h = din("c_ident", [128, 128], BF16)
    triu_h = din("c_triu", [128, 128], BF16)
    ones_h = din("c_ones", [1, 512], BF16)
    out_h = nc.dram_tensor("out", [2048, 1024], F32, kind="ExternalOutput").ap()

    es = ExitStack()
    with es:
        def sbt(name, shape, dtype):
            return es.enter_context(nc.sbuf_tensor(name, shape, dtype))

        ST = sbt("ST", [128, NTILES * 1024], BF16)
        XHT = sbt("XHT", [128, NTILES * 1024], BF16)
        WE0 = sbt("WE0", [128, 4096], BF16)
        WE1 = sbt("WE1", [128, 4096], BF16)
        WQ = sbt("WQ", [128, 8192], BF16)
        WO = sbt("WO", [128, 8192], BF16)
        WKE = sbt("WKE", [128, 2048], BF16)
        WVE = sbt("WVE", [128, 2048], BF16)
        WCE = sbt("WCE", [128, 512], BF16)
        WCT = sbt("WCT", [64, 256], BF16)
        WES = sbt("WES", [64, 256], BF16)
        RWT = sbt("RWT", [128, 64], BF16)
        EBR = [sbt("EBR0", [1, 1024], BF16), sbt("EBR1", [1, 1024], BF16)]
        BQ = sbt("BQ", [1, 1024], BF16)
        BO = sbt("BO", [1, 1024], BF16)
        RB = sbt("RB", [1, 8], BF16)
        BKE = sbt("BKE", [1, 256], BF16)
        BVE = sbt("BVE", [1, 256], BF16)
        BKC = sbt("BKC", [128, 2], BF16)
        BVC = sbt("BVC", [128, 2], BF16)
        GB = sbt("GB", [128, 1024], BF16)
        BB = sbt("BB", [128, 1024], BF16)
        IDE = sbt("IDE", [128, 128], BF16)
        TRIU = sbt("TRIU", [128, 128], BF16)
        ONES = sbt("ONES", [1, 512], BF16)
        XH = sbt("XH", [128, 4096], BF16)
        QFT = sbt("QFT", [128, 4096], BF16)
        KFT = sbt("KFT", [128, 1024], BF16)
        KFP = sbt("KFP", [128, 2048], BF16)
        VOS = sbt("VOS", [128, 1040], BF16)
        KTOK = sbt("KTOK", [128, 256], BF16)
        SMB = sbt("SMB", [128, 512], BF16)
        MZF = sbt("MZF", [128, 260], F32)
        MZB = sbt("MZB", [128, 260], BF16)
        OSB = sbt("OSB", [128, 1024], BF16)
        OT = sbt("OT", [128, 1024], BF16)
        YSB = sbt("YSB", [128, 1024], F32)
        ZSB = sbt("ZSB", [128, 1024], BF16)
        TSB = sbt("TSB", [128, 1024], F32)
        FSC = sbt("FSC", [128, 512], F32)
        FSC2 = sbt("FSC2", [128, 512], F32)
        FSC3 = sbt("FSC3", [128, 512], F32)
        STT = sbt("STT", [128, 48], F32)
        MV = sbt("MV", [128, 8], F32)
        RST = sbt("RST", [128, 4], F32)
        NMR = sbt("NMR", [128, 4], F32)
        LMX = sbt("LMX", [128, 4], F32)
        EXPO = sbt("EXPO", [128, 32], F32)
        ZR = sbt("ZR", [128, 4], F32)
        RZ2 = sbt("RZ2", [128, 4], F32)
        MEQ = sbt("MEQ", [128, 32], F32)
        CSU = sbt("CSU", [128, 32], F32)
        SEL = sbt("SEL", [128, 32], F32)
        T2 = sbt("T2", [128, 32], F32)
        ONESF = sbt("ONESF", [128, 8], F32)
        MPP = sbt("MPP", [128, 128], F32)
        DEN = sbt("DEN", [128, 16], F32)
        RCP = sbt("RCP", [128, 16], F32)
        YST = sbt("YST", [128, 12], F32)
        YMV = sbt("YMV", [128, 2], F32)
        YRS = sbt("YRS", [128, 1], F32)
        YNM = sbt("YNM", [128, 1], F32)

        pf = [es.enter_context(nc.psum_tensor(f"pf{i}", [128, 512], F32)) for i in range(8)]

        sIN = Ctr(es.enter_context(nc.semaphore("sIN")))
        sX = Ctr(es.enter_context(nc.semaphore("sX")))
        sWS = Ctr(es.enter_context(nc.semaphore("sWS")))
        sWA = Ctr(es.enter_context(nc.semaphore("sWA")))
        sWB = Ctr(es.enter_context(nc.semaphore("sWB")))
        sWE = Ctr(es.enter_context(nc.semaphore("sWE")))
        sPE = Ctr(es.enter_context(nc.semaphore("sPE")))
        sVE = Ctr(es.enter_context(nc.semaphore("sVE")))
        sAC = Ctr(es.enter_context(nc.semaphore("sAC")))
        sPO = Ctr(es.enter_context(nc.semaphore("sPO")))
        sOUT = Ctr(es.enter_context(nc.semaphore("sOUT")))

        block = es.enter_context(nc.Block())
        M = {}

        def gate_ffn(site, e, h):
            return f"gate_{site}_e{e}_h{h}"

        setup_loads = ["wcT", "wee", "bkc", "bvc", "rwT", "rb", "wkT", "wvT",
                       "gam", "bet", "bq", "wq", "wo", "bo"]
        wsched = [("x", None, sX)] + [(k, None, sWS) for k in setup_loads]
        for site in ("ffn1", "ffn2"):
            for e in range(E):
                for h in range(2):
                    if site == "ffn1":
                        gate = "gate_setup" if e == 0 else gate_ffn("ffn1", e - 1, h)
                    else:
                        gate = gate_ffn("ffn1", 7, h) if e == 0 else gate_ffn("ffn2", e - 1, h)
                    wsched.append((f"{site}_ew_e{e}_h{h}", gate, sWA if h == 0 else sWB))
                    if h == 0:
                        wsched.append((f"{site}_eb_e{e}", gate, sWE))
        w_after = {}
        chain_acc = {}
        for key, _, sem in wsched:
            chain_acc[id(sem)] = chain_acc.get(id(sem), 0) + 16
            w_after[key] = (sem, chain_acc[id(sem)])
        WS_ALL = chain_acc[id(sWS)]
        part_a = []
        part_b = []
        for item in wsched:
            key = item[0]
            if key.startswith("ffn2_") and not (key.endswith("_e0_h0") or key.endswith("_e0_h1")
                                                or key.endswith("_eb_e0")):
                part_b.append(item)
            else:
                part_a.append(item)

        src_map = {
            "x": (x_h.rearrange("(i p) f -> p i f", p=128),
                  ST[:].rearrange("p (i f) -> p i f", i=NTILES)),
            "wcT": (wcT_h, WCT[:]),
            "wee": (wee_h, WES[:]),
            "bkc": (bkc_h, BKC[:]),
            "bvc": (bvc_h, BVC[:]),
            "rwT": (rwT_h.rearrange("(dc p) e -> p dc e", p=128),
                    RWT[:].rearrange("p (dc e) -> p dc e", e=8)),
            "rb": (rb_h, RB[:]),
            "wkT": (wkT_h.rearrange("(cc p) d -> p cc d", p=128),
                    WE0[:, 0:2048].rearrange("p (cc d) -> p cc d", cc=2)),
            "wvT": (wvT_h.rearrange("(cc p) d -> p cc d", p=128),
                    WE1[:, 0:2048].rearrange("p (cc d) -> p cc d", cc=2)),
            "gam": (gam_h, EBR[0][:]),
            "bet": (bet_h, EBR[1][:]),
            "bq": (bq_h, BQ[:]),
            "wq": (wq_h.rearrange("(dc p) f -> p dc f", p=128),
                   WQ[:].rearrange("p (dc f) -> p dc f", dc=8)),
            "wo": (wo_h.rearrange("(dc p) f -> p dc f", p=128),
                   WO[:].rearrange("p (dc f) -> p dc f", dc=8)),
            "bo": (bo_h, BO[:]),
        }

        def emit_wdma(items):
            @block.gpsimd
            def _(gpsimd):
                for key, gate, sem in items:
                    if gate is not None:
                        gpsimd.wait_ge(sPE.h, M[gate])
                    if key in src_map:
                        src, dst = src_map[key]
                        gpsimd.dma_start(out=dst, in_=src).then_inc(sem.h, 16)
                    elif "_ew_" in key:
                        parts = key.split("_")
                        e = int(parts[2][1:]); h = int(parts[3][1:])
                        buf = WE0 if (2 * e + h) % 2 == 0 else WE1
                        gpsimd.dma_start(
                            out=buf[:].rearrange("p (dc o) -> p dc o", dc=8),
                            in_=ew_h[e].rearrange("(dc p) o -> p dc o", p=128)[
                                :, :, 512 * h:512 * (h + 1)],
                        ).then_inc(sem.h, 16)
                    else:
                        parts = key.split("_")
                        e = int(parts[2][1:])
                        gpsimd.dma_start(out=EBR[e % 2][:],
                                         in_=eb_h[e:e + 1, :]).then_inc(sem.h, 16)
                    sem.v += 16
                    assert (w_after[key][0] is sem and sem.v == w_after[key][1]), key

        @block.sync
        def _(sync):
            sync.dma_start(out=IDE[:], in_=ident_h).then_inc(sIN.h, 16)
            sync.dma_start(out=TRIU[:], in_=triu_h).then_inc(sIN.h, 16)
            sync.dma_start(out=ONES[:], in_=ones_h).then_inc(sIN.h, 16)
        sIN.v = 48
        M["in_done"] = sIN.v
        

        @block.vector
        def _(vector):
            sVE.tick(vector.memset(KFP[:], 0.0))
            sVE.tick(vector.memset(MZF[:], 0.0))
            sVE.tick(vector.memset(MZB[:], 0.0))
            v = sVE.tick(vector.memset(VOS[:], 0.0))
            sVE.tick(vector.memset(ONESF[:], 1.0))
            vector.wait_ge(sVE.h, v)
            vector.memset(
                VOS[:].rearrange("p (c k o) -> p c k o", c=4, k=4)[:, :, :, 64:65], 1.0)
            sVE.dtick(vector)
        M["init_ve"] = sVE.v

        _p = sVE.v
        M["gb_ve"] = _p + 2
        M["bb_ve"] = _p + 4
        M["wce_ve"] = _p + 5
        M["bke_ve"] = _p + 6
        M["bve_ve"] = _p + 7
        for jn, j in (("wke", 5), ("wve", 6)):
            for dc in range(NDC):
                M[f"{jn}_dc{dc}_ve"] = _p + 8 + (0 if jn == "wke" else NDC) + dc

        @block.tensor
        def _(tensor):
            tensor.wait_ge(sIN.h, M["in_done"])
            tensor.wait_ge(sWS.h, WS_ALL)
            for hh in range(2):
                mm = tensor.matmul(pf[hh][:], ONES[0:1, 0:128],
                                   EBR[0][0:1, 512 * hh:512 * (hh + 1)], start=True, stop=True)
            M["gbB_pe"] = sPE.tick(mm)
            tensor.wait_ge(sVE.h, M["gb_ve"])
            for hh in range(2):
                mm = tensor.matmul(pf[hh][:], ONES[0:1, 0:128],
                                   EBR[1][0:1, 512 * hh:512 * (hh + 1)], start=True, stop=True)
            M["bbB_pe"] = sPE.tick(mm)
            for cc in range(2):
                mm = tensor.matmul(pf[2][:, 256 * cc:256 * (cc + 1)],
                                   WCT[0:64, 128 * cc:128 * (cc + 1)], WES[0:64, :],
                                   start=True, stop=True)
            M["wce_pe"] = sPE.tick(mm)
            tensor.wait_ge(sVE.h, M["wce_ve"])
            for j, bc in ((3, BKC), (4, BVC)):
                for cc in range(2):
                    mm = tensor.matmul(pf[j][0:1, 0:256], bc[:, cc:cc + 1],
                                       WCE[:, 256 * cc:256 * (cc + 1)],
                                       start=(cc == 0), stop=(cc == 1))
            M["beff_pe"] = sPE.tick(mm)
            bank_prev = {0: None, 1: None}
            for jn, wsrc in (("wke", WE0), ("wve", WE1)):
                for dc in range(NDC):
                    bsel = dc % 2
                    if bank_prev[bsel] is not None:
                        tensor.wait_ge(sVE.h, M[bank_prev[bsel]])
                    bank = pf[5 + bsel]
                    for cc in range(2):
                        mm = tensor.matmul(bank[:, 0:256],
                                           wsrc[:, 1024 * cc + 128 * dc:1024 * cc + 128 * (dc + 1)],
                                           WCE[:, 256 * cc:256 * (cc + 1)],
                                           start=(cc == 0), stop=(cc == 1))
                    M[f"{jn}_dc{dc}_pe"] = sPE.tick(mm)
                    bank_prev[bsel] = f"{jn}_dc{dc}_ve"
            M["gate_setup"] = sPE.v

        @block.vector
        def _(vector):
            vector.wait_ge(sPE.h, M["gbB_pe"])
            vector.tensor_copy(GB[:, 0:512], pf[0][:])
            sVE.dtick(vector)
            vector.tensor_copy(GB[:, 512:1024], pf[1][:])
            v = sVE.dtick(vector)
            assert v == M["gb_ve"]
            vector.wait_ge(sPE.h, M["bbB_pe"])
            vector.tensor_copy(BB[:, 0:512], pf[0][:])
            sVE.dtick(vector)
            vector.tensor_copy(BB[:, 512:1024], pf[1][:])
            v = sVE.dtick(vector)
            assert v == M["bb_ve"]
            vector.wait_ge(sPE.h, M["wce_pe"])
            vector.tensor_copy(WCE[:], pf[2][:])
            v = sVE.dtick(vector)
            assert v == M["wce_ve"]
            vector.wait_ge(sPE.h, M["beff_pe"])
            vector.tensor_copy(BKE[:], pf[3][0:1, 0:256])
            sVE.dtick(vector)
            vector.tensor_copy(BVE[:], pf[4][0:1, 0:256])
            sVE.dtick(vector)
            for jn, wdst in (("wke", WKE), ("wve", WVE)):
                for dc in range(NDC):
                    vector.wait_ge(sPE.h, M[f"{jn}_dc{dc}_pe"])
                    bank = pf[5 + dc % 2]
                    vector.tensor_copy(wdst[:, 256 * dc:256 * (dc + 1)], bank[:, 0:256])
                    v = sVE.dtick(vector)
                    assert v == M[f"{jn}_dc{dc}_ve"], (jn, dc, v)
        M["setup_ve"] = sVE.v

        def tile_cols(i, h=None):
            if h is None:
                return slice(1024 * i, 1024 * (i + 1))
            return slice(1024 * i + 512 * h, 1024 * i + 512 * (h + 1))

        def xht_lhsT(i, dc):
            return XHT[:, 1024 * i + 128 * dc:1024 * i + 128 * (dc + 1)]

        def xht_mov(st, dc):
            return XHT[:].rearrange("p (i r) -> p i r", i=NTILES)[
                :, 4 * st:4 * st + 4, 128 * dc:128 * (dc + 1)]

        pp_state = {"n": 0}

        def emit_prepass_group(tiles, site, stats_ve_wait=None, pb0_ve_wait=None,
                               pb0_ac_wait=None):
            g = pp_state["n"]
            pp_state["n"] += 1
            is_ffn = site in ("ffn1", "ffn2")

            @block.vector
            def _(vector):
                vector.wait_ge(sIN.h, M["in_done"])
                vector.wait_ge(sX.h, 16)
                if g > 0:
                    vector.wait_ge(sAC.h, M[f"pp{g-1}_apply_ac"])
                if stats_ve_wait:
                    vector.wait_ge(sVE.h, stats_ve_wait)
                if g > 0:
                    vector.wait_ge(sVE.h, M[f"pp{g-1}_nmr_ve"])
                v = 0
                for j, i in enumerate(tiles):
                    sVE.tick(vector.bn_stats(STT[:, 12 * j:12 * j + 6], ST[:, tile_cols(i, 0)]))
                    v = sVE.tick(vector.bn_stats(STT[:, 12 * j + 6:12 * j + 12],
                                                 ST[:, tile_cols(i, 1)]))
                vector.wait_ge(sVE.h, v)
                for j in range(4):
                    v = sVE.tick(vector.bn_aggr(MV[:, 2 * j:2 * j + 2],
                                                STT[:, 12 * j:12 * j + 12]))
                vector.wait_ge(sVE.h, v)
                vector.tensor_scalar_add(
                    RST[:].rearrange("p (a b) -> p a b", b=1),
                    MV[:].rearrange("p (a b) -> p a b", b=2)[:, :, 1:2], LN_EPS)
                M[f"pp{g}_rec_ve"] = sVE.dtick(vector)

            @block.scalar
            def _(scalar):
                scalar.wait_ge(sVE.h, M[f"pp{g}_rec_ve"])
                a = sAC.tick(scalar.activation(out=RST[:], in_=RST[:], func=AF.Ln))
                scalar.wait_ge(sAC.h, a)
                scalar.activation(out=RST[:], in_=RST[:], func=AF.Exp, scale=-0.5)
                M[f"pp{g}_sqrt_ac"] = sAC.dtick(scalar)

            @block.vector
            def _(vector):
                vector.wait_ge(sAC.h, M[f"pp{g}_sqrt_ac"])
                vector.scalar_tensor_tensor(
                    NMR[:],
                    MV[:].rearrange("p (a b) -> p a b", b=2)[:, :, 0:1].rearrange(
                        "p a b -> p (a b)"),
                    -1.0, RST[:], op0=ALU.mult, op1=ALU.mult)
                M[f"pp{g}_nmr_ve"] = sVE.dtick(vector)

            @block.scalar
            def _(scalar):
                scalar.wait_ge(sVE.h, M[f"pp{g}_nmr_ve"])
                if g > 0:
                    scalar.wait_ge(sPE.h, M[f"pp{g-1}_tp_pe"])
                for j, i in enumerate(tiles):
                    scalar.activation(
                        out=XH[:, 1024 * j:1024 * (j + 1)], in_=ST[:, tile_cols(i)],
                        func=AF.Identity, bias=NMR[:, j:j + 1], scale=RST[:, j:j + 1])
                    a = sAC.dtick(scalar)
                    M[f"pp_apply_t{i}_ac"] = a
                M[f"pp{g}_apply_ac"] = a

            _vb = sVE.v
            _ab = sAC.v
            for j, i in enumerate(tiles):
                M[f"pp_evac_t{i}_ve"] = _vb + j + 1
                M[f"pp_evac_t{i}_ac"] = _ab + j + 1

            @block.tensor
            def _(tensor):
                if pb0_ve_wait:
                    tensor.wait_ge(sVE.h, pb0_ve_wait)
                if pb0_ac_wait:
                    tensor.wait_ge(sAC.h, pb0_ac_wait)
                tensor.wait_ge(sIN.h, M["in_done"])
                if g > 0:
                    tensor.wait_ge(sVE.h, M[f"pp{g-1}_evac_ve"])
                    tensor.wait_ge(sAC.h, M[f"pp{g-1}_evac_ac"])
                for j, i in enumerate(tiles):
                    tensor.wait_ge(sAC.h, M[f"pp_apply_t{i}_ac"])
                    if j > 0:
                        tensor.wait_ge(sVE.h, M[f"pp_evac_t{tiles[j-1]}_ve"])
                    for q in range(4):
                        mm = tensor.matmul(
                            pf[0][:, 128 * q:128 * (q + 1)],
                            XH[:, 1024 * j + 128 * q:1024 * j + 128 * (q + 1)],
                            IDE[:], start=True, stop=True)
                    mm.then_inc(sPE.h, 1)
                    sPE.v += 1
                    M[f"pp_tpA_t{i}_pe"] = sPE.v
                    if j > 0:
                        tensor.wait_ge(sAC.h, M[f"pp_evac_t{tiles[j-1]}_ac"])
                    for q in range(4):
                        mm = tensor.matmul(
                            pf[1][:, 128 * q:128 * (q + 1)],
                            XH[:, 1024 * j + 128 * (q + 4):1024 * j + 128 * (q + 5)],
                            IDE[:], start=True, stop=True)
                    M[f"pp_tp_t{i}_pe"] = sPE.tick(mm)
                M[f"pp{g}_tp_pe"] = sPE.v

            @block.vector
            def _(vector):
                for j, i in enumerate(tiles):
                    vector.wait_ge(sPE.h, M[f"pp_tpA_t{i}_pe"])
                    vector.tensor_copy(XHT[:, 1024 * i:1024 * i + 512], pf[0][:])
                    v = sVE.dtick(vector)
                    assert v == M[f"pp_evac_t{i}_ve"]
                M[f"pp{g}_evac_ve"] = sVE.v

            @block.scalar
            def _(scalar):
                for j, i in enumerate(tiles):
                    scalar.wait_ge(sPE.h, M[f"pp_tp_t{i}_pe"])
                    scalar.activation(
                        out=XHT[:, 1024 * i + 512:1024 * (i + 1)], in_=pf[1][:],
                        func=AF.Identity)
                    a = sAC.dtick(scalar)
                    assert a == M[f"pp_evac_t{i}_ac"]
                M[f"pp{g}_evac_ac"] = sAC.v

            if not is_ffn:
                return

            @block.tensor
            def _(tensor):
                tensor.wait_ge(sVE.h, M[f"pp{g}_evac_ve"])
                tensor.wait_ge(sAC.h, M[f"pp{g}_evac_ac"])
                tensor.wait_ge(sWS.h, WS_ALL)
                tensor.wait_ge(sVE.h, M.get("pf4_free_ve", M["setup_ve"]))
                tensor.wait_ge(sVE.h, M["setup_ve"])
                if M.get("mpp_busy_ve") is not None:
                    tensor.wait_ge(sVE.h, M["mpp_busy_ve"])
                if M.get("last_router_meq_ve") is not None:
                    tensor.wait_ge(sVE.h, M["last_router_meq_ve"])
                    tensor.wait_ge(sAC.h, M["last_router_expo_ac"])
                for j, i in enumerate(tiles):
                    for dc in range(NDC):
                        tensor.matmul(pf[4][:, 8 * j:8 * (j + 1)], xht_lhsT(i, dc),
                                      RWT[:, 8 * dc:8 * (dc + 1)],
                                      start=(dc == 0), stop=False)
                    mm = tensor.matmul(pf[4][:, 8 * j:8 * (j + 1)], ONES[0:1, 0:128],
                                       RB[0:1, :], start=False, stop=True)
                M[f"pp{g}_router_pe"] = sPE.tick(mm)

            @block.scalar
            def _(scalar):
                scalar.wait_ge(sPE.h, M[f"pp{g}_router_pe"])
                if M.get("last_router_mpp_ve") is not None:
                    scalar.wait_ge(sVE.h, M["last_router_mpp_ve"])
                scalar.activation(out=EXPO[:], in_=pf[4][:, 0:32], func=AF.Exp)
                a = sAC.dtick(scalar)
                M[f"pp{g}_expo_ac"] = a
                M["last_router_expo_ac"] = a

            @block.vector
            def _(vector):
                vector.wait_ge(sPE.h, M[f"pp{g}_router_pe"])
                v = sVE.tick(vector.tensor_reduce(
                    LMX[:].rearrange("p (a b) -> p a b", b=1),
                    pf[4][:, 0:32].rearrange("p (a b) -> p a b", b=8),
                    axis=AX.X, op=ALU.max))
                vector.wait_ge(sVE.h, v)
                for j in range(4):
                    v = sVE.tick(vector.tensor_scalar(
                        MEQ[:, 8 * j:8 * (j + 1)], pf[4][:, 8 * j:8 * (j + 1)],
                        LMX[:, j:j + 1], None, op0=ALU.is_equal))
                M["last_router_meq_ve"] = v
                vector.wait_ge(sVE.h, v)
                for j in range(4):
                    v = sVE.tick(vector.tensor_tensor_scan(
                        CSU[:, 8 * j:8 * (j + 1)], MEQ[:, 8 * j:8 * (j + 1)],
                        ONESF[:], 0.0, op0=ALU.add, op1=ALU.mult))
                vector.wait_ge(sVE.h, v)
                v = sVE.tick(vector.tensor_scalar(SEL[:], CSU[:], 1.0, None, op0=ALU.is_equal))
                vector.wait_ge(sVE.h, v)
                v = sVE.tick(vector.tensor_mul(SEL[:], SEL[:], MEQ[:]))
                vector.wait_ge(sAC.h, M[f"pp{g}_expo_ac"])
                vector.wait_ge(sVE.h, v)
                v = sVE.tick(vector.tensor_mul(T2[:], SEL[:], EXPO[:]))
                vector.wait_ge(sVE.h, v)
                v = sVE.tick(vector.tensor_reduce(
                    ZR[:].rearrange("p (a b) -> p a b", b=1),
                    EXPO[:].rearrange("p (a b) -> p a b", b=8),
                    axis=AX.X, op=ALU.add))
                vector.wait_ge(sVE.h, v)
                v = sVE.tick(vector.reciprocal(ZR[:], ZR[:]))
                vector.wait_ge(sVE.h, v)
                v = sVE.tick(vector.tensor_scalar_mul(RZ2[:], ZR[:], 0.5))
                vector.wait_ge(sVE.h, v)
                for j, i in enumerate(tiles):
                    v = sVE.tick(vector.tensor_scalar_mul(
                        MPP[:, 8 * i:8 * (i + 1)], T2[:, 8 * j:8 * (j + 1)],
                        RZ2[:, j:j + 1]))
                M[f"pp{g}_mpp_ve"] = sVE.v
                M["last_router_mpp_ve"] = sVE.v

        def emit_ffn(site):
            g0 = pp_state["n"]
            for gg in range(4):
                tiles = list(range(4 * gg, 4 * gg + 4))
                stats_wait = None
                pb0_ve = pb0_ac = None
                if site == "ffn2" and gg == 0:
                    pb0_ve = M["sc15_y_ve"]
                    pb0_ac = M["nb3_free_ac"]
                if site == "ffn1" and gg == 0:
                    pb0_ve = M["bb_ve"]
                emit_prepass_group(tiles, site, stats_ve_wait=stats_wait,
                                   pb0_ve_wait=pb0_ve, pb0_ac_wait=pb0_ac)
            xht_ve = M[f"pp{g0+3}_evac_ve"]
            xht_ac = M[f"pp{g0+3}_evac_ac"]
            mpp_ready = M[f"pp{g0+3}_mpp_ve"]

            if trunc == "prepass" and site == "ffn1":
                M[f"{site}_stt_done_ve"] = M[f"pp{g0+3}_mpp_ve"]
                for i in range(NTILES):
                    for h in range(2):
                        M[f"{site}_stt_t{i}_h{h}_ve"] = M[f"pp{g0+3}_mpp_ve"]
                M[f"{site}_pe_done"] = sPE.v
                for e in range(E):
                    for h in range(2):
                        M[f"gate_{site}_e{e}_h{h}"] = sPE.v
                M["mpp_busy_ve"] = M[f"{site}_stt_done_ve"]
                return
            groups = [(e, h, i) for e in range(E) for h in range(2) for i in range(NTILES)]
            base = sVE.v
            for gi in range(len(groups)):
                M[f"{site}_stt_g{gi}_ve"] = base + gi + 1

            @block.tensor
            def _(tensor):
                tensor.wait_ge(sVE.h, xht_ve)
                tensor.wait_ge(sAC.h, xht_ac)
                if site == "ffn1":
                    tensor.wait_ge(sVE.h, M["setup_ve"])
                else:
                    tensor.wait_ge(sVE.h, M["sc15_y_ve"])
                    tensor.wait_ge(sAC.h, M["nb2_free_ac"])
                    tensor.wait_ge(sAC.h, M["nb3_free_ac"])
                for gi, (e, h, i) in enumerate(groups):
                    bank = pf[gi % 3]
                    if gi >= 3:
                        tensor.wait_ge(sVE.h, M[f"{site}_stt_g{gi-3}_ve"])
                    if i == 0:
                        _s1, _v1 = w_after[f"{site}_ew_e{e}_h{h}"]
                        tensor.wait_ge(_s1.h, _v1)
                        _s2, _v2 = w_after[f"{site}_eb_e{e}"]
                        tensor.wait_ge(_s2.h, _v2)
                    buf = WE0 if (2 * e + h) % 2 == 0 else WE1
                    for dc in range(NDC):
                        tensor.matmul(bank[:], xht_lhsT(i, dc),
                                      buf[:, 512 * dc:512 * (dc + 1)],
                                      start=(dc == 0), stop=False)
                    mm = tensor.matmul(bank[:], ONES[0:1, 0:128],
                                       EBR[e % 2][0:1, 512 * h:512 * (h + 1)],
                                       start=False, stop=True)
                    p = sPE.tick(mm)
                    M[f"{site}_mm_g{gi}_pe"] = p
                    if i == NTILES - 1:
                        M[f"gate_{site}_e{e}_h{h}"] = p
                M[f"{site}_pe_done"] = sPE.v

            @block.vector
            def _(vector):
                vector.wait_ge(sVE.h, mpp_ready)
                vector.wait_ge(sX.h, 16)
                if site == "ffn2":
                    vector.wait_ge(sPO.h, M["attn_done_po"])
                for gi, (e, h, i) in enumerate(groups):
                    bank = pf[gi % 3]
                    vector.wait_ge(sPE.h, M[f"{site}_mm_g{gi}_pe"])
                    if gi >= 32:
                        vector.wait_ge(sVE.h, M[f"{site}_stt_g{gi-32}_ve"])
                    if e == 0:
                        vector.wait_ge(sAC.h, M[f"pp_apply_t{i}_ac"])
                    v = sVE.tick(vector.scalar_tensor_tensor(
                        ST[:, tile_cols(i, h)], bank[:], MPP[:, 8 * i + e:8 * i + e + 1],
                        ST[:, tile_cols(i, h)], op0=ALU.mult, op1=ALU.add))
                    assert v == M[f"{site}_stt_g{gi}_ve"]
                    if e == E - 1:
                        M[f"{site}_stt_t{i}_h{h}_ve"] = v
                M[f"{site}_stt_done_ve"] = sVE.dtick(vector)
            M["mpp_busy_ve"] = M[f"{site}_stt_done_ve"]

        def emit_feat_evac(tag, bank, dst, dst_cols, pe_target, prev_ve):
            @block.scalar
            def _(scalar):
                scalar.wait_ge(sPE.h, pe_target)
                if prev_ve:
                    scalar.wait_ge(sVE.h, prev_ve)
                a = sAC.tick(scalar.activation(out=FSC[:], in_=bank[:], func=AF.Relu,
                                               scale=-1.0))
                scalar.wait_ge(sAC.h, a)
                scalar.activation(out=FSC2[:], in_=FSC[:], func=AF.Exp, scale=-1.0)
                M[f"{tag}_ac"] = sAC.dtick(scalar)

            @block.vector
            def _(vector):
                vector.wait_ge(sPE.h, pe_target)
                if prev_ve:
                    vector.wait_ge(sVE.h, prev_ve)
                v = sVE.tick(vector.tensor_scalar_max(FSC3[:], bank[:], 0.0))
                vector.wait_ge(sVE.h, v)
                vector.wait_ge(sAC.h, M[f"{tag}_ac"])
                vector.tensor_add(dst[:, dst_cols], FSC2[:], FSC3[:])
                M[f"{tag}_ve"] = sVE.dtick(vector)

        def emit_scan_chunk(st, c):
            i = 4 * st + c
            cg = i

            @block.tensor
            def _(tensor):
                tensor.wait_ge(sVE.h, M[f"at{st}_kfp_ve"])
                tensor.wait_ge(sVE.h, M[f"at{st}_vos_ve"])
                if cg > 0:
                    tensor.wait_ge(sAC.h, M[f"sc{cg-1}_otev_ac"])
                for fc in range(2):
                    mm = tensor.matmul(pf[7][:, 128 * fc:128 * (fc + 1)],
                                       KFT[:, 512 * fc + 128 * c:512 * fc + 128 * (c + 1)],
                                       IDE[:], start=True, stop=True)
                M[f"sc{cg}_ktokT_pe"] = sPE.tick(mm)

            @block.vector
            def _(vector):
                vector.wait_ge(sPE.h, M[f"sc{cg}_ktokT_pe"])
                vector.tensor_copy(KTOK[:], pf[7][:, 0:256])
                M[f"sc{cg}_ktok_ve"] = sVE.dtick(vector)

            bank_last_mask = {}
            for kv in range(4):
                heads = list(range(4 * kv, 4 * kv + 4))
                nb_i = 2 + (kv % 2)
                nbank = pf[nb_i]

                @block.tensor
                def _(tensor):
                    if kv == 0:
                        tensor.wait_ge(sVE.h, M[f"at{st}_qev_fc7_ve"])
                        if cg > 0:
                            tensor.wait_ge(sVE.h, M[f"sc{cg-1}_y_ve"])
                    for jj, hq in enumerate(heads):
                        sb_i = jj % 2
                        sbank = pf[sb_i]
                        fcq, _bp = q_slot(hq)
                        if bank_last_mask.get(sb_i) is not None:
                            tensor.wait_ge(sVE.h, bank_last_mask[sb_i])
                        mm = tensor.matmul(
                            sbank[:, 0:128],
                            KFP[:, 512 * kv + 128 * c:512 * kv + 128 * (c + 1)],
                            QFT[:, 512 * fcq + 128 * c:512 * fcq + 128 * (c + 1)],
                            start=True, stop=True)
                        M[f"sc{cg}_kv{kv}_s{jj}_pe"] = sPE.tick(mm)

                _mb = sVE.v
                for jj in range(4):
                    M[f"sc{cg}_kv{kv}_sm{jj}_ve"] = _mb + jj + 1

                @block.vector
                def _(vector):
                    if M.get(f"smslots_busy_kv{kv}_pe") is not None:
                        vector.wait_ge(sPE.h, M[f"smslots_busy_kv{kv}_pe"])
                    for jj, hq in enumerate(heads):
                        vector.wait_ge(sPE.h, M[f"sc{cg}_kv{kv}_s{jj}_pe"])
                        v = sVE.tick(vector.tensor_tensor(
                            SMB[:, 128 * jj:128 * (jj + 1)], pf[jj % 2][:, 0:128], TRIU[:],
                            op=ALU.mult))
                        assert v == M[f"sc{cg}_kv{kv}_sm{jj}_ve"]
                        bank_last_mask[jj % 2] = v
                    M[f"sc{cg}_kv{kv}_smD_ve"] = sVE.dtick(vector)

                @block.tensor
                def _(tensor):
                    if M.get(f"nb{nb_i}_free_ac") is not None:
                        tensor.wait_ge(sAC.h, M[f"nb{nb_i}_free_ac"])
                    tensor.wait_ge(sVE.h, M[f"at{st}_vos_ve"])
                    if cg > 0:
                        tensor.wait_ge(sVE.h, M[f"sc{cg-1}_mzb_ve"])
                    tensor.wait_ge(sVE.h, M[f"sc{cg}_kv{kv}_smD_ve"])
                    for jj, hq in enumerate(heads):
                        fcq, _bp = q_slot(hq)
                        tensor.matmul(nbank[:, 65 * jj:65 * jj + 65],
                                      SMB[:, 128 * jj:128 * (jj + 1)],
                                      VOS[:, 260 * c + 65 * kv:260 * c + 65 * (kv + 1)],
                                      start=True, stop=False)
                        mm = tensor.matmul(nbank[:, 65 * jj:65 * jj + 65],
                                           QFT[:, 512 * fcq + 128 * c:512 * fcq + 128 * (c + 1)],
                                           MZB[:, 65 * kv:65 * (kv + 1)],
                                           start=False, stop=True)
                    M[f"sc{cg}_kv{kv}_num_pe"] = sPE.tick(mm)
                    M[f"smslots_busy_kv{(kv + 1) % 4}_pe"] = sPE.v

                @block.vector
                def _(vector):
                    vector.wait_ge(sPE.h, M[f"sc{cg}_kv{kv}_num_pe"])
                    v = sVE.tick(vector.tensor_scalar_add(
                        DEN[:, 4 * kv:4 * kv + 4].rearrange("p (a b) -> p a b", b=1),
                        nbank[:, 0:260].rearrange("p (a b) -> p a b", b=65)[:, :, 64:65],
                        DEN_EPS))
                    vector.wait_ge(sVE.h, v)
                    vector.reciprocal(RCP[:, 4 * kv:4 * kv + 4],
                                      DEN[:, 4 * kv:4 * kv + 4])
                    M[f"sc{cg}_kv{kv}_rcp_ve"] = sVE.dtick(vector)

                @block.scalar
                def _(scalar):
                    scalar.wait_ge(sVE.h, M[f"sc{cg}_kv{kv}_rcp_ve"])
                    if cg > 0:
                        scalar.wait_ge(sPE.h, M[f"sc{cg-1}_oT_pe"])
                    for jj, hq in enumerate(heads):
                        scalar.activation(
                            out=OSB[:, 64 * hq:64 * (hq + 1)],
                            in_=nbank[:, 65 * jj:65 * jj + 64],
                            func=AF.Identity, scale=RCP[:, 4 * kv + jj:4 * kv + jj + 1])
                        if jj < 3:
                            sAC.tick(scalar.nop())
                    a = sAC.dtick(scalar)
                    M[f"nb{nb_i}_free_ac"] = a
                    M[f"sc{cg}_kv{kv}_div_ac"] = a

            @block.tensor
            def _(tensor):
                tensor.wait_ge(sVE.h, M[f"sc{cg}_ktok_ve"])
                if cg > 0:
                    tensor.wait_ge(sVE.h, M[f"sc{cg-1}_mzf_ve"])
                else:
                    tensor.wait_ge(sVE.h, M["setup_ve"])
                for kv in range(4):
                    par = kv % 2
                    bank = pf[5 + par]
                    colo = 65 * (kv // 2)
                    mm = tensor.matmul(bank[64 * par:64 * par + 64, colo:colo + 65],
                                       KTOK[:, 128 * (kv // 2) + 64 * par:
                                            128 * (kv // 2) + 64 * par + 64],
                                       VOS[:, 260 * c + 65 * kv:260 * c + 65 * (kv + 1)],
                                       start=True, stop=True,
                                       tile_position=(0, 64 * par), skip_group_check=True)
                M[f"sc{cg}_state_pe"] = sPE.tick(mm)

            @block.vector
            def _(vector):
                vector.wait_ge(sPE.h, M[f"sc{cg}_state_pe"])
                if cg == 0:
                    vector.wait_ge(sVE.h, M["init_ve"])
                v = 0
                for kv in range(4):
                    par = kv % 2
                    rows = slice(64 * par, 64 * par + 64)
                    colo = 65 * (kv // 2)
                    v = sVE.tick(vector.tensor_tensor(
                        MZF[rows, 65 * kv:65 * (kv + 1)],
                        pf[5 + par][rows, colo:colo + 65],
                        MZF[rows, 65 * kv:65 * (kv + 1)], op=ALU.add))
                vector.wait_ge(sVE.h, v)
                M[f"sc{cg}_mzf_ve"] = v
                for kv in range(4):
                    par = kv % 2
                    rows = slice(64 * par, 64 * par + 64)
                    vector.tensor_copy(MZB[rows, 65 * kv:65 * (kv + 1)],
                                       MZF[rows, 65 * kv:65 * (kv + 1)])
                    if kv < 3:
                        sVE.tick(vector.nop())
                M[f"sc{cg}_mzb_ve"] = sVE.dtick(vector)

            M[f"sc{cg}_oTA_pred_ve"] = sVE.v + 1

            @block.tensor
            def _(tensor):
                for kv in range(4):
                    tensor.wait_ge(sAC.h, M[f"sc{cg}_kv{kv}_div_ac"])
                tensor.wait_ge(sVE.h, M[f"sc{cg}_ktok_ve"])
                for q in range(4):
                    mm = tensor.matmul(pf[7][:, 128 * q:128 * (q + 1)],
                                       OSB[:, 128 * q:128 * (q + 1)], IDE[:],
                                       start=True, stop=True)
                mm.then_inc(sPE.h, 1)
                sPE.v += 1
                M[f"sc{cg}_oTA_pe"] = sPE.v
                tensor.wait_ge(sVE.h, M[f"sc{cg}_oTA_pred_ve"])
                for q in range(4):
                    mm = tensor.matmul(pf[7][:, 128 * q:128 * (q + 1)],
                                       OSB[:, 128 * (q + 4):128 * (q + 5)], IDE[:],
                                       start=True, stop=True)
                M[f"sc{cg}_oT_pe"] = sPE.tick(mm)

            @block.vector
            def _(vector):
                vector.wait_ge(sPE.h, M[f"sc{cg}_oTA_pe"])
                if cg > 0:
                    vector.wait_ge(sPE.h, M[f"sc{cg-1}_yo_h1_pe"])
                vector.tensor_copy(OT[:, 0:512], pf[7][:])
                v = sVE.dtick(vector)
                assert v == M[f"sc{cg}_oTA_pred_ve"]
                M[f"sc{cg}_otev_ve"] = v

            @block.scalar
            def _(scalar):
                scalar.wait_ge(sPE.h, M[f"sc{cg}_oT_pe"])
                if cg > 0:
                    scalar.wait_ge(sPE.h, M[f"sc{cg-1}_yo_h1_pe"])
                scalar.activation(out=OT[:, 512:1024], in_=pf[7][:], func=AF.Identity)
                M[f"sc{cg}_otev_ac"] = sAC.dtick(scalar)

            @block.tensor
            def _(tensor):
                tensor.wait_ge(sVE.h, M[f"sc{cg}_otev_ve"])
                tensor.wait_ge(sAC.h, M[f"sc{cg}_otev_ac"])
                tensor.wait_ge(sWS.h, WS_ALL)
                for h in range(2):
                    bank = pf[h]
                    for oc in range(NDC):
                        tensor.matmul(bank[:], OT[:, 128 * oc:128 * (oc + 1)],
                                      WO[:, 1024 * oc + 512 * h:1024 * oc + 512 * (h + 1)],
                                      start=(oc == 0), stop=False)
                    mm = tensor.matmul(bank[:], ONES[0:1, 0:128],
                                       BO[0:1, 512 * h:512 * (h + 1)],
                                       start=False, stop=True)
                    M[f"sc{cg}_yo_h{h}_pe"] = sPE.tick(mm)

            @block.vector
            def _(vector):
                v = 0
                vector.wait_ge(sAC.h, M[f"pp_apply_t{i}_ac"])
                for h in range(2):
                    vector.wait_ge(sPE.h, M[f"sc{cg}_yo_h{h}_pe"])
                    if cg > 0 and h == 0:
                        vector.wait_ge(sAC.h, M[f"sc{cg-1}_z_ac"])
                    v = sVE.tick(vector.tensor_add(
                        YSB[:, 512 * h:512 * (h + 1)], pf[h][:],
                        XH[:, 1024 * c + 512 * h:1024 * c + 512 * (h + 1)]))
                M[f"sc{cg}_y_ve"] = v
                vector.wait_ge(sVE.h, v)
                vector.bn_stats(YST[:, 0:6], YSB[:, 0:512])
                v = sVE.tick(vector.bn_stats(YST[:, 6:12], YSB[:, 512:1024]))
                vector.wait_ge(sVE.h, v)
                v = sVE.tick(vector.bn_aggr(YMV[:], YST[:]))
                vector.wait_ge(sVE.h, v)
                vector.tensor_scalar_add(YRS[:], YMV[:, 1:2], LN_EPS)
                M[f"sc{cg}_yrs_ve"] = sVE.dtick(vector)

            @block.scalar
            def _(scalar):
                scalar.wait_ge(sVE.h, M[f"sc{cg}_yrs_ve"])
                a = sAC.tick(scalar.activation(out=YRS[:], in_=YRS[:], func=AF.Ln))
                scalar.wait_ge(sAC.h, a)
                scalar.activation(out=YRS[:], in_=YRS[:], func=AF.Exp, scale=-0.5)
                M[f"sc{cg}_ysqrt_ac"] = sAC.dtick(scalar)

            @block.vector
            def _(vector):
                vector.wait_ge(sAC.h, M[f"sc{cg}_ysqrt_ac"])
                vector.scalar_tensor_tensor(YNM[:], YMV[:, 0:1], -1.0, YRS[:],
                                            op0=ALU.mult, op1=ALU.mult)
                M[f"sc{cg}_ynm_ve"] = sVE.dtick(vector)

            @block.scalar
            def _(scalar):
                scalar.wait_ge(sVE.h, M[f"sc{cg}_ynm_ve"])
                if cg > 0:
                    scalar.wait_ge(sPO.h, M[f"sc{cg-1}_tsb1_po"])
                scalar.activation(out=ZSB[:], in_=YSB[:], func=AF.Identity,
                                  bias=YNM[:], scale=YRS[:])
                M[f"sc{cg}_z_ac"] = sAC.dtick(scalar)

            @block.gpsimd
            def _(gpsimd):
                gpsimd.wait_ge(sAC.h, M[f"sc{cg}_z_ac"])
                gpsimd.wait_ge(sVE.h, M["ffn1_stt_done_ve"])
                p = sPO.tick(gpsimd.tensor_tensor(TSB[:], ZSB[:], GB[:], op=ALU.mult))
                M[f"sc{cg}_tsb1_po"] = p
                gpsimd.wait_ge(sPO.h, p)
                p = sPO.tick(gpsimd.tensor_tensor(TSB[:], TSB[:], BB[:], op=ALU.add))
                gpsimd.wait_ge(sPO.h, p)
                p = sPO.tick(gpsimd.tensor_tensor(ST[:, tile_cols(i)], ST[:, tile_cols(i)],
                                                  TSB[:], op=ALU.add))
                M[f"attn_st_t{i}_po"] = p

        def emit_attn():
            for st in range(4):
                tiles = list(range(4 * st, 4 * st + 4))
                g = pp_state["n"]
                stats_wait = max(M[f"ffn1_stt_t{i}_h{h}_ve"] for i in tiles for h in range(2))
                if st > 0:
                    stats_wait = max(stats_wait, M[f"sc{4*st-1}_y_ve"])
                pb0_ve = M[f"sc{4*st-1}_y_ve"] if st > 0 else M["ffn1_stt_done_ve"]
                pb0_ac = M[f"sc{4*st-1}_otev_ac"] if st > 0 else None
                emit_prepass_group(tiles, "attn", stats_ve_wait=stats_wait,
                                   pb0_ve_wait=pb0_ve, pb0_ac_wait=pb0_ac)
                xht_ve = M[f"pp{g}_evac_ve"]
                xht_ac = M[f"pp{g}_evac_ac"]

                _vb = sVE.v
                for fc in range(NDC):
                    M[f"at{st}_qfp{fc}_ve"] = _vb + 2 * (fc + 1)

                @block.tensor
                def _(tensor):
                    tensor.wait_ge(sVE.h, xht_ve)
                    tensor.wait_ge(sAC.h, xht_ac)
                    tensor.wait_ge(sWS.h, WS_ALL)
                    if st == 0:
                        tensor.wait_ge(sVE.h, M["setup_ve"])
                        tensor.wait_ge(sVE.h, M["init_ve"])
                        tensor.wait_ge(sIN.h, M["in_done"])
                    else:
                        tensor.wait_ge(sVE.h, M[f"sc{4*st-1}_y_ve"])
                    for fc in range(NDC):
                        bank = pf[fc % 2]
                        if fc >= 2:
                            tensor.wait_ge(sVE.h, M[f"at{st}_qfp{fc-2}_ve"])
                        for dc in range(NDC):
                            tensor.matmul(bank[:],
                                          WQ[:, 1024 * dc + 128 * fc:1024 * dc + 128 * (fc + 1)],
                                          xht_mov(st, dc), start=(dc == 0), stop=False)
                        mm = tensor.matmul(bank[:], BQ[0:1, 128 * fc:128 * (fc + 1)],
                                           ONES[0:1, 0:512], start=False, stop=True)
                        M[f"at{st}_q_fc{fc}_pe"] = sPE.tick(mm)

                prev = M[f"at{st-1}_kev_fc1_ve"] if st > 0 else None
                for fc in range(NDC):
                    emit_feat_evac(f"at{st}_qev_fc{fc}", pf[fc % 2], QFT,
                                   slice(512 * fc, 512 * (fc + 1)),
                                   M[f"at{st}_q_fc{fc}_pe"], prev)
                    prev = M[f"at{st}_qev_fc{fc}_ve"]
                    assert prev == M[f"at{st}_qfp{fc}_ve"]
                M[f"at{st}_qev_fc7_ve"] = prev

                @block.tensor
                def _(tensor):
                    if st > 0:
                        tensor.wait_ge(sAC.h, M["nb2_free_ac"])
                        tensor.wait_ge(sAC.h, M["nb3_free_ac"])
                    for fc in range(2):
                        bank = pf[2 + fc]
                        for dc in range(NDC):
                            tensor.matmul(bank[:],
                                          WKE[:, 256 * dc + 128 * fc:256 * dc + 128 * (fc + 1)],
                                          xht_mov(st, dc), start=(dc == 0), stop=False)
                        mm = tensor.matmul(bank[:], BKE[0:1, 128 * fc:128 * (fc + 1)],
                                           ONES[0:1, 0:512], start=False, stop=True)
                        M[f"at{st}_k_fc{fc}_pe"] = sPE.tick(mm)

                prevk = prev
                for fc in range(2):
                    emit_feat_evac(f"at{st}_kev_fc{fc}", pf[2 + fc], KFT,
                                   slice(512 * fc, 512 * (fc + 1)),
                                   M[f"at{st}_k_fc{fc}_pe"], prevk)
                    prevk = M[f"at{st}_kev_fc{fc}_ve"]

                @block.vector
                def _(vector):
                    if st > 0:
                        vector.wait_ge(sPE.h, M[f"sc{4*st-1}_kv3_num_pe"])
                    for kv in range(4):
                        b = 64 * (kv % 2)
                        fc = kv // 2
                        vector.wait_ge(sVE.h, M[f"at{st}_kev_fc{fc}_ve"])
                        vector.tensor_copy(
                            KFP[b:b + 64, 512 * kv:512 * (kv + 1)],
                            KFT[b:b + 64, 512 * fc:512 * (fc + 1)])
                        v = sVE.tick(vector.engine_nop()) if False else sVE.dtick(vector)
                    M[f"at{st}_kfp_ve"] = v

                for cc in range(4):
                    M[f"at{st}_vev_pred_c{cc}"] = sVE.v + cc + 1

                @block.tensor
                def _(tensor):
                    if st == 0:
                        tensor.wait_ge(sVE.h, M["ffn1_rtr_free_ve"])
                    for cc in range(4):
                        ii = 4 * st + cc
                        if st > 0 or cc > 0:
                            tensor.wait_ge(sVE.h, M["vev_prev_ve"])
                        if st > 0 and cc == 0:
                            tensor.wait_ge(sPE.h, M[f"sc{4*st-1}_state_pe"])
                            tensor.wait_ge(sPE.h, M[f"sc{4*st-1}_kv3_num_pe"])
                        for dc in range(NDC):
                            tensor.matmul(pf[4][:, 0:256], xht_lhsT(ii, dc),
                                          WVE[:, 256 * dc:256 * (dc + 1)],
                                          start=(dc == 0), stop=False)
                        mm = tensor.matmul(pf[4][:, 0:256], ONES[0:1, 0:128],
                                           BVE[0:1, 0:256], start=False, stop=True)
                        M[f"at{st}_v_c{cc}_pe"] = sPE.tick(mm)
                        M["vev_prev_ve"] = M[f"at{st}_vev_pred_c{cc}"]

                @block.vector
                def _(vector):
                    if st == 0:
                        vector.wait_ge(sVE.h, M["init_ve"])
                    for cc in range(4):
                        vector.wait_ge(sPE.h, M[f"at{st}_v_c{cc}_pe"])
                        dst = VOS[:, 260 * cc:260 * (cc + 1)].rearrange(
                            "p (k o) -> p k o", k=4)[:, :, 0:64]
                        src = pf[4][:, 0:256].rearrange("p (k o) -> p k o", k=4)
                        v = sVE.tick(vector.tensor_copy(dst, src))
                        assert v == M[f"at{st}_vev_pred_c{cc}"]
                    M[f"at{st}_vos_ve"] = sVE.dtick(vector)

                for cc in range(4):
                    emit_scan_chunk(st, cc)

            M["attn_last_otev_ve"] = M["sc15_otev_ve"]
            M["attn_last_otev_ac"] = M["sc15_otev_ac"]

            @block.gpsimd
            def _(gpsimd):
                sPO.dtick(gpsimd)
            M["attn_done_po"] = sPO.v
            M["pf4_free_ve"] = M["at3_vos_ve"]

        if trunc == "setup":
            emit_wdma(part_a[:15])
            @block.gpsimd
            def _(gpsimd):
                gpsimd.wait_ge(sVE.h, M["setup_ve"])
                gpsimd.dma_start(
                    out=out_h.rearrange("(i p) f -> p i f", p=128),
                    in_=ST[:].rearrange("p (i f) -> p i f", i=NTILES),
                ).then_inc(sOUT.h, 16)
                gpsimd.wait_ge(sOUT.h, 16)
            return nc
        if "ffn1" in stages:
            emit_ffn("ffn1")
            M["ffn1_rtr_free_ve"] = M["ffn1_stt_done_ve"]

        emit_wdma(part_a)

        if "attn" in stages:
            emit_attn()

        if "ffn2" in stages:
            @block.vector
            def _(vector):
                vector.wait_ge(sPO.h, M["attn_done_po"])

            @block.scalar
            def _(scalar):
                scalar.wait_ge(sPO.h, M["attn_done_po"])
            emit_ffn("ffn2")

        if "ffn2" in stages:
            emit_wdma(part_b)

        last = "ffn2" if "ffn2" in stages else ("attn" if "attn" in stages else "ffn1")

        @block.gpsimd
        def _(gpsimd):
            if last == "ffn2":
                gpsimd.wait_ge(sVE.h, M["ffn2_stt_done_ve"])
            elif last == "attn":
                gpsimd.wait_ge(sPO.h, M["attn_done_po"])
            else:
                gpsimd.wait_ge(sVE.h, M["ffn1_stt_done_ve"])
            gpsimd.dma_start(
                out=out_h.rearrange("(i p) f -> p i f", p=128),
                in_=ST[:].rearrange("p (i f) -> p i f", i=NTILES),
            ).then_inc(sOUT.h, 16)
            sOUT.v += 16
            gpsimd.wait_ge(sOUT.h, sOUT.v)

    return nc


def prep_inputs(d):
    f32 = np.float32
    x = np.ascontiguousarray(np.asarray(d["L0_attn_x"], f32))
    wq = np.ascontiguousarray(np.asarray(d["L0_attn_Wq"], f32)[:, QPERM])
    bq = np.ascontiguousarray(np.asarray(d["L0_attn_bq"], f32)[QPERM][None, :])
    wkT = np.ascontiguousarray(np.asarray(d["L0_attn_Wk"], f32).T)
    wvT = np.ascontiguousarray(np.asarray(d["L0_attn_Wv"], f32).T)
    wcT = np.ascontiguousarray(np.asarray(d["L0_attn_Wc"], f32).T)
    wee = np.ascontiguousarray(np.asarray(d["L0_attn_We"], f32))
    bkc = np.ascontiguousarray(np.asarray(d["L0_attn_bk"], f32).reshape(2, 128).T)
    bvc = np.ascontiguousarray(np.asarray(d["L0_attn_bv"], f32).reshape(2, 128).T)
    wo = np.ascontiguousarray(np.asarray(d["L0_attn_Wo"], f32))
    bo = np.ascontiguousarray(np.asarray(d["L0_attn_bo"], f32)[None, :])
    gam = np.ascontiguousarray(np.asarray(d["L0_attn_gamma"], f32)[None, :])
    bet = np.ascontiguousarray(np.asarray(d["L0_attn_beta"], f32)[None, :])
    rwT = np.ascontiguousarray(np.asarray(d["L0_ffn_router_w"], f32).T)
    rb = np.ascontiguousarray(np.asarray(d["L0_ffn_router_b"], f32)[None, :])
    ew = np.ascontiguousarray(np.asarray(d["L0_ffn_expert_w"], f32).transpose(0, 2, 1))
    eb = np.ascontiguousarray(np.asarray(d["L0_ffn_expert_b"], f32))
    c_ident = np.eye(128, dtype=f32).astype(NBF)
    c_triu = np.triu(np.ones((128, 128), f32)).astype(NBF)
    c_ones = np.ones((1, 512), f32).astype(NBF)
    shared = dict(wq=wq, bq=bq, wkT=wkT, wvT=wvT, wcT=wcT, wee=wee, bkc=bkc,
                  bvc=bvc, wo=wo, bo=bo, gam=gam, bet=bet, rwT=rwT, rb=rb,
                  ew=ew, eb=eb, c_ident=c_ident, c_triu=c_triu, c_ones=c_ones)
    in_maps = []
    for b in range(8):
        m = dict(shared)
        m["x"] = np.ascontiguousarray(x[b])
        in_maps.append(m)
    return in_maps


def kernel(**inputs):
    in_maps = prep_inputs(inputs)
    nc = build_nc()
    res = run_bass_kernel_spmd(nc, in_maps, core_ids=list(range(8)))
    out = np.stack([res.results[b]["out"] for b in range(8)], axis=0)
    return out.astype(np.float32)


# version 02
# speedup vs baseline: 1.0005x; 1.0005x over previous
import numpy as np
import ml_dtypes
from contextlib import ExitStack

import concourse.bass as bass
import concourse.mybir as mybir
from concourse.bass_utils import run_bass_kernel_spmd

F32 = mybir.dt.float32
BF16 = mybir.dt.bfloat16
AF = mybir.ActivationFunctionType
ALU = mybir.AluOpType
AX = mybir.AxisListType
NBF = ml_dtypes.bfloat16

NTILES = 16
NDC = 8
E = 8
LN_EPS = 1e-5
DEN_EPS = 1e-6

QPAIRS = [(0, 4), (1, 5), (2, 6), (3, 7), (8, 12), (9, 13), (10, 14), (11, 15)]
QPERM = np.array([h * 64 + d for pair in QPAIRS for h in pair for d in range(64)])


def q_slot(hq):
    fcq = hq % 4 + 4 * (hq // 8)
    base = 64 * ((hq // 4) % 2)
    assert QPAIRS[fcq][base // 64] == hq
    return fcq, base


class Ctr:
    def __init__(self, h):
        self.h = h
        self.v = 0

    def tick(self, inst, k=1):
        inst.then_inc(self.h, k)
        self.v += k
        return self.v

    def mark(self, eng, k=1):
        eng.sem_inc(self.h, k)
        self.v += k
        return self.v

    def dtick(self, eng, k=1):
        eng.drain().then_inc(self.h, k)
        self.v += k
        return self.v


def build_nc(stages=("ffn1", "attn", "ffn2")):
    import os
    trunc = os.environ.get("KTRUNC", "")
    nc = bass.Bass("TRN2", target_bir_lowering=False)

    def din(name, shape, dtype=F32):
        return nc.dram_tensor(name, shape, dtype, kind="ExternalInput").ap()

    x_h = din("x", [2048, 1024])
    wq_h = din("wq", [1024, 1024])
    bq_h = din("bq", [1, 1024])
    wkT_h = din("wkT", [256, 1024])
    wvT_h = din("wvT", [256, 1024])
    wcT_h = din("wcT", [64, 256])
    wee_h = din("wee", [64, 256])
    bkc_h = din("bkc", [128, 2])
    bvc_h = din("bvc", [128, 2])
    wo_h = din("wo", [1024, 1024])
    bo_h = din("bo", [1, 1024])
    gam_h = din("gam", [1, 1024])
    bet_h = din("bet", [1, 1024])
    rwT_h = din("rwT", [1024, 8])
    rb_h = din("rb", [1, 8])
    ew_h = din("ew", [8, 1024, 1024])
    eb_h = din("eb", [8, 1024])
    ident_h = din("c_ident", [128, 128], BF16)
    triu_h = din("c_triu", [128, 128], BF16)
    ones_h = din("c_ones", [1, 512], BF16)
    out_h = nc.dram_tensor("out", [2048, 1024], F32, kind="ExternalOutput").ap()

    es = ExitStack()
    with es:
        def sbt(name, shape, dtype):
            return es.enter_context(nc.sbuf_tensor(name, shape, dtype))

        ST = sbt("ST", [128, NTILES * 1024], BF16)
        XHT = sbt("XHT", [128, NTILES * 1024], BF16)
        WE0 = sbt("WE0", [128, 4096], BF16)
        WE1 = sbt("WE1", [128, 4096], BF16)
        WQ = sbt("WQ", [128, 8192], BF16)
        WO = sbt("WO", [128, 8192], BF16)
        WKE = sbt("WKE", [128, 2048], BF16)
        WVE = sbt("WVE", [128, 2048], BF16)
        WCE = sbt("WCE", [128, 512], BF16)
        WCT = sbt("WCT", [64, 256], BF16)
        WES = sbt("WES", [64, 256], BF16)
        RWT = sbt("RWT", [128, 64], BF16)
        EBR = [sbt("EBR0", [1, 1024], BF16), sbt("EBR1", [1, 1024], BF16)]
        BQ = sbt("BQ", [1, 1024], BF16)
        BO = sbt("BO", [1, 1024], BF16)
        RB = sbt("RB", [1, 8], BF16)
        BKE = sbt("BKE", [1, 256], BF16)
        BVE = sbt("BVE", [1, 256], BF16)
        BKC = sbt("BKC", [128, 2], BF16)
        BVC = sbt("BVC", [128, 2], BF16)
        GB = sbt("GB", [128, 1024], BF16)
        BB = sbt("BB", [128, 1024], BF16)
        IDE = sbt("IDE", [128, 128], BF16)
        TRIU = sbt("TRIU", [128, 128], BF16)
        ONES = sbt("ONES", [1, 512], BF16)
        XH = sbt("XH", [128, 4096], BF16)
        QFT = sbt("QFT", [128, 4096], BF16)
        KFT = sbt("KFT", [128, 1024], BF16)
        KFP = sbt("KFP", [128, 2048], BF16)
        VOS = sbt("VOS", [128, 1040], BF16)
        KTOK = sbt("KTOK", [128, 256], BF16)
        SMB = sbt("SMB", [128, 512], BF16)
        MZF = sbt("MZF", [128, 260], F32)
        MZB = sbt("MZB", [128, 260], BF16)
        OSB = sbt("OSB", [128, 1024], BF16)
        OT = sbt("OT", [128, 1024], BF16)
        YSB = sbt("YSB", [128, 1024], F32)
        ZSB = sbt("ZSB", [128, 1024], BF16)
        TSB = sbt("TSB", [128, 1024], F32)
        FSC = sbt("FSC", [128, 512], F32)
        FSC2 = sbt("FSC2", [128, 512], F32)
        FSC3 = sbt("FSC3", [128, 512], F32)
        STTb = [sbt(f"STT{p}", [128, 48], F32) for p in range(2)]
        MVb = [sbt(f"MV{p}", [128, 8], F32) for p in range(2)]
        RSTb = [sbt(f"RST{p}", [128, 4], F32) for p in range(2)]
        NMRb = [sbt(f"NMR{p}", [128, 4], F32) for p in range(2)]
        LMXb = [sbt(f"LMX{p}", [128, 4], F32) for p in range(2)]
        EXPOb = [sbt(f"EXPO{p}", [128, 32], F32) for p in range(2)]
        ZRb = [sbt(f"ZR{p}", [128, 4], F32) for p in range(2)]
        RZ2b = [sbt(f"RZ2{p}", [128, 4], F32) for p in range(2)]
        MEQb = [sbt(f"MEQ{p}", [128, 32], F32) for p in range(2)]
        CSUb = [sbt(f"CSU{p}", [128, 32], F32) for p in range(2)]
        SELb = [sbt(f"SEL{p}", [128, 32], F32) for p in range(2)]
        T2b = [sbt(f"T2{p}", [128, 32], F32) for p in range(2)]
        ONESF = sbt("ONESF", [128, 8], F32)
        MPP = sbt("MPP", [128, 128], F32)
        DEN = sbt("DEN", [128, 16], F32)
        RCP = sbt("RCP", [128, 16], F32)
        YST = sbt("YST", [128, 12], F32)
        YMV = sbt("YMV", [128, 2], F32)
        YRS = sbt("YRS", [128, 1], F32)
        YNM = sbt("YNM", [128, 1], F32)

        pf = [es.enter_context(nc.psum_tensor(f"pf{i}", [128, 512], F32)) for i in range(8)]

        sIN = Ctr(es.enter_context(nc.semaphore("sIN")))
        sX = Ctr(es.enter_context(nc.semaphore("sX")))
        sWS = Ctr(es.enter_context(nc.semaphore("sWS")))
        sWA = Ctr(es.enter_context(nc.semaphore("sWA")))
        sWB = Ctr(es.enter_context(nc.semaphore("sWB")))
        sWE = Ctr(es.enter_context(nc.semaphore("sWE")))
        sPE = Ctr(es.enter_context(nc.semaphore("sPE")))
        sVE = Ctr(es.enter_context(nc.semaphore("sVE")))
        sAC = Ctr(es.enter_context(nc.semaphore("sAC")))
        sPO = Ctr(es.enter_context(nc.semaphore("sPO")))
        sOUT = Ctr(es.enter_context(nc.semaphore("sOUT")))

        block = es.enter_context(nc.Block())
        M = {}

        def gate_ffn(site, e, h):
            return f"gate_{site}_e{e}_h{h}"

        setup_loads = ["wcT", "wee", "bkc", "bvc", "rwT", "rb", "wkT", "wvT",
                       "gam", "bet", "bq", "wq", "wo", "bo"]
        wsched = [("x", None, sX)] + [(k, None, sWS) for k in setup_loads]
        for site in ("ffn1", "ffn2"):
            for e in range(E):
                for h in range(2):
                    if site == "ffn1":
                        gate = "gate_setup" if e == 0 else gate_ffn("ffn1", e - 1, h)
                    else:
                        gate = gate_ffn("ffn1", 7, h) if e == 0 else gate_ffn("ffn2", e - 1, h)
                    wsched.append((f"{site}_ew_e{e}_h{h}", gate, sWA if h == 0 else sWB))
                    if h == 0:
                        wsched.append((f"{site}_eb_e{e}", gate, sWE))
        w_after = {}
        chain_acc = {}
        for key, _, sem in wsched:
            chain_acc[id(sem)] = chain_acc.get(id(sem), 0) + 16
            w_after[key] = (sem, chain_acc[id(sem)])
        WS_ALL = chain_acc[id(sWS)]
        part_a = []
        part_b = []
        for item in wsched:
            key = item[0]
            if key.startswith("ffn2_") and not (key.endswith("_e0_h0") or key.endswith("_e0_h1")
                                                or key.endswith("_eb_e0")):
                part_b.append(item)
            else:
                part_a.append(item)

        src_map = {
            "x": (x_h.rearrange("(i p) f -> p i f", p=128),
                  ST[:].rearrange("p (i f) -> p i f", i=NTILES)),
            "wcT": (wcT_h, WCT[:]),
            "wee": (wee_h, WES[:]),
            "bkc": (bkc_h, BKC[:]),
            "bvc": (bvc_h, BVC[:]),
            "rwT": (rwT_h.rearrange("(dc p) e -> p dc e", p=128),
                    RWT[:].rearrange("p (dc e) -> p dc e", e=8)),
            "rb": (rb_h, RB[:]),
            "wkT": (wkT_h.rearrange("(cc p) d -> p cc d", p=128),
                    WE0[:, 0:2048].rearrange("p (cc d) -> p cc d", cc=2)),
            "wvT": (wvT_h.rearrange("(cc p) d -> p cc d", p=128),
                    WE1[:, 0:2048].rearrange("p (cc d) -> p cc d", cc=2)),
            "gam": (gam_h, EBR[0][:]),
            "bet": (bet_h, EBR[1][:]),
            "bq": (bq_h, BQ[:]),
            "wq": (wq_h.rearrange("(dc p) f -> p dc f", p=128),
                   WQ[:].rearrange("p (dc f) -> p dc f", dc=8)),
            "wo": (wo_h.rearrange("(dc p) f -> p dc f", p=128),
                   WO[:].rearrange("p (dc f) -> p dc f", dc=8)),
            "bo": (bo_h, BO[:]),
        }

        def emit_wdma(items):
            @block.gpsimd
            def _(gpsimd):
                for key, gate, sem in items:
                    if gate is not None:
                        gpsimd.wait_ge(sPE.h, M[gate])
                    if key in src_map:
                        src, dst = src_map[key]
                        gpsimd.dma_start(out=dst, in_=src).then_inc(sem.h, 16)
                    elif "_ew_" in key:
                        parts = key.split("_")
                        e = int(parts[2][1:]); h = int(parts[3][1:])
                        buf = WE0 if (2 * e + h) % 2 == 0 else WE1
                        gpsimd.dma_start(
                            out=buf[:].rearrange("p (dc o) -> p dc o", dc=8),
                            in_=ew_h[e].rearrange("(dc p) o -> p dc o", p=128)[
                                :, :, 512 * h:512 * (h + 1)],
                        ).then_inc(sem.h, 16)
                    else:
                        parts = key.split("_")
                        e = int(parts[2][1:])
                        gpsimd.dma_start(out=EBR[e % 2][:],
                                         in_=eb_h[e:e + 1, :]).then_inc(sem.h, 16)
                    sem.v += 16
                    assert (w_after[key][0] is sem and sem.v == w_after[key][1]), key

        @block.sync
        def _(sync):
            sync.dma_start(out=IDE[:], in_=ident_h).then_inc(sIN.h, 16)
            sync.dma_start(out=TRIU[:], in_=triu_h).then_inc(sIN.h, 16)
            sync.dma_start(out=ONES[:], in_=ones_h).then_inc(sIN.h, 16)
        sIN.v = 48
        M["in_done"] = sIN.v
        

        @block.vector
        def _(vector):
            sVE.tick(vector.memset(KFP[:], 0.0))
            sVE.tick(vector.memset(MZF[:], 0.0))
            sVE.tick(vector.memset(MZB[:], 0.0))
            v = sVE.tick(vector.memset(VOS[:], 0.0))
            sVE.tick(vector.memset(ONESF[:], 1.0))
            vector.wait_ge(sVE.h, v)
            vector.memset(
                VOS[:].rearrange("p (c k o) -> p c k o", c=4, k=4)[:, :, :, 64:65], 1.0)
            sVE.dtick(vector)
        M["init_ve"] = sVE.v

        _p = sVE.v
        M["gb_ve"] = _p + 2
        M["bb_ve"] = _p + 4
        M["wce_ve"] = _p + 5
        M["bke_ve"] = _p + 6
        M["bve_ve"] = _p + 7
        for jn, j in (("wke", 5), ("wve", 6)):
            for dc in range(NDC):
                M[f"{jn}_dc{dc}_ve"] = _p + 8 + (0 if jn == "wke" else NDC) + dc

        @block.tensor
        def _(tensor):
            tensor.wait_ge(sIN.h, M["in_done"])
            tensor.wait_ge(sWS.h, WS_ALL)
            for hh in range(2):
                mm = tensor.matmul(pf[hh][:], ONES[0:1, 0:128],
                                   EBR[0][0:1, 512 * hh:512 * (hh + 1)], start=True, stop=True)
            M["gbB_pe"] = sPE.tick(mm)
            tensor.wait_ge(sVE.h, M["gb_ve"])
            for hh in range(2):
                mm = tensor.matmul(pf[hh][:], ONES[0:1, 0:128],
                                   EBR[1][0:1, 512 * hh:512 * (hh + 1)], start=True, stop=True)
            M["bbB_pe"] = sPE.tick(mm)
            for cc in range(2):
                mm = tensor.matmul(pf[2][:, 256 * cc:256 * (cc + 1)],
                                   WCT[0:64, 128 * cc:128 * (cc + 1)], WES[0:64, :],
                                   start=True, stop=True)
            M["wce_pe"] = sPE.tick(mm)
            tensor.wait_ge(sVE.h, M["wce_ve"])
            for j, bc in ((3, BKC), (4, BVC)):
                for cc in range(2):
                    mm = tensor.matmul(pf[j][0:1, 0:256], bc[:, cc:cc + 1],
                                       WCE[:, 256 * cc:256 * (cc + 1)],
                                       start=(cc == 0), stop=(cc == 1))
            M["beff_pe"] = sPE.tick(mm)
            bank_prev = {0: None, 1: None}
            for jn, wsrc in (("wke", WE0), ("wve", WE1)):
                for dc in range(NDC):
                    bsel = dc % 2
                    if bank_prev[bsel] is not None:
                        tensor.wait_ge(sVE.h, M[bank_prev[bsel]])
                    bank = pf[5 + bsel]
                    for cc in range(2):
                        mm = tensor.matmul(bank[:, 0:256],
                                           wsrc[:, 1024 * cc + 128 * dc:1024 * cc + 128 * (dc + 1)],
                                           WCE[:, 256 * cc:256 * (cc + 1)],
                                           start=(cc == 0), stop=(cc == 1))
                    M[f"{jn}_dc{dc}_pe"] = sPE.tick(mm)
                    bank_prev[bsel] = f"{jn}_dc{dc}_ve"
            M["gate_setup"] = sPE.v

        @block.vector
        def _(vector):
            vector.wait_ge(sPE.h, M["gbB_pe"])
            vector.tensor_copy(GB[:, 0:512], pf[0][:])
            sVE.dtick(vector)
            vector.tensor_copy(GB[:, 512:1024], pf[1][:])
            v = sVE.dtick(vector)
            assert v == M["gb_ve"]
            vector.wait_ge(sPE.h, M["bbB_pe"])
            vector.tensor_copy(BB[:, 0:512], pf[0][:])
            sVE.dtick(vector)
            vector.tensor_copy(BB[:, 512:1024], pf[1][:])
            v = sVE.dtick(vector)
            assert v == M["bb_ve"]
            vector.wait_ge(sPE.h, M["wce_pe"])
            vector.tensor_copy(WCE[:], pf[2][:])
            v = sVE.dtick(vector)
            assert v == M["wce_ve"]
            vector.wait_ge(sPE.h, M["beff_pe"])
            vector.tensor_copy(BKE[:], pf[3][0:1, 0:256])
            sVE.dtick(vector)
            vector.tensor_copy(BVE[:], pf[4][0:1, 0:256])
            sVE.dtick(vector)
            for jn, wdst in (("wke", WKE), ("wve", WVE)):
                for dc in range(NDC):
                    vector.wait_ge(sPE.h, M[f"{jn}_dc{dc}_pe"])
                    bank = pf[5 + dc % 2]
                    vector.tensor_copy(wdst[:, 256 * dc:256 * (dc + 1)], bank[:, 0:256])
                    v = sVE.dtick(vector)
                    assert v == M[f"{jn}_dc{dc}_ve"], (jn, dc, v)
        M["setup_ve"] = sVE.v

        def tile_cols(i, h=None):
            if h is None:
                return slice(1024 * i, 1024 * (i + 1))
            return slice(1024 * i + 512 * h, 1024 * i + 512 * (h + 1))

        def xht_lhsT(i, dc):
            return XHT[:, 1024 * i + 128 * dc:1024 * i + 128 * (dc + 1)]

        def xht_mov(st, dc):
            return XHT[:].rearrange("p (i r) -> p i r", i=NTILES)[
                :, 4 * st:4 * st + 4, 128 * dc:128 * (dc + 1)]

        pp_state = {"n": 0}

        def emit_prepass_group(tiles, site, stats_ve_wait=None, pb0_ve_wait=None,
                               pb0_ac_wait=None, apply_ve_wait=None):
            g = pp_state["n"]
            pp_state["n"] += 1
            is_ffn = site in ("ffn1", "ffn2")
            par = g % 2
            STT, MV, RST, NMR = STTb[par], MVb[par], RSTb[par], NMRb[par]
            LMX, EXPO, ZR, RZ2 = LMXb[par], EXPOb[par], ZRb[par], RZ2b[par]
            MEQ, CSU, SEL, T2 = MEQb[par], CSUb[par], SELb[par], T2b[par]

            @block.vector
            def _(vector):
                vector.wait_ge(sIN.h, M["in_done"])
                vector.wait_ge(sX.h, 16)
                if g > 1:
                    vector.wait_ge(sAC.h, M[f"pp{g-2}_apply_ac"])
                if stats_ve_wait:
                    vector.wait_ge(sVE.h, stats_ve_wait)
                v = 0
                for j, i in enumerate(tiles):
                    sVE.tick(vector.bn_stats(STT[:, 12 * j:12 * j + 6], ST[:, tile_cols(i, 0)]))
                    v = sVE.tick(vector.bn_stats(STT[:, 12 * j + 6:12 * j + 12],
                                                 ST[:, tile_cols(i, 1)]))
                vector.wait_ge(sVE.h, v)
                for j in range(4):
                    v = sVE.tick(vector.bn_aggr(MV[:, 2 * j:2 * j + 2],
                                                STT[:, 12 * j:12 * j + 12]))
                vector.wait_ge(sVE.h, v)
                vector.tensor_scalar_add(
                    RST[:].rearrange("p (a b) -> p a b", b=1),
                    MV[:].rearrange("p (a b) -> p a b", b=2)[:, :, 1:2], LN_EPS)
                M[f"pp{g}_rec_ve"] = sVE.dtick(vector)

            @block.scalar
            def _(scalar):
                scalar.wait_ge(sVE.h, M[f"pp{g}_rec_ve"])
                a = sAC.tick(scalar.activation(out=RST[:], in_=RST[:], func=AF.Ln))
                scalar.wait_ge(sAC.h, a)
                scalar.activation(out=RST[:], in_=RST[:], func=AF.Exp, scale=-0.5)
                M[f"pp{g}_sqrt_ac"] = sAC.dtick(scalar)

            @block.vector
            def _(vector):
                vector.wait_ge(sAC.h, M[f"pp{g}_sqrt_ac"])
                vector.scalar_tensor_tensor(
                    NMR[:],
                    MV[:].rearrange("p (a b) -> p a b", b=2)[:, :, 0:1].rearrange(
                        "p a b -> p (a b)"),
                    -1.0, RST[:], op0=ALU.mult, op1=ALU.mult)
                M[f"pp{g}_nmr_ve"] = sVE.dtick(vector)

            @block.scalar
            def _(scalar):
                scalar.wait_ge(sVE.h, M[f"pp{g}_nmr_ve"])
                if apply_ve_wait:
                    scalar.wait_ge(sVE.h, apply_ve_wait)
                if g > 0:
                    scalar.wait_ge(sPE.h, M[f"pp{g-1}_tp_pe"])
                for j, i in enumerate(tiles):
                    scalar.activation(
                        out=XH[:, 1024 * j:1024 * (j + 1)], in_=ST[:, tile_cols(i)],
                        func=AF.Identity, bias=NMR[:, j:j + 1], scale=RST[:, j:j + 1])
                    a = sAC.dtick(scalar)
                    M[f"pp_apply_t{i}_ac"] = a
                M[f"pp{g}_apply_ac"] = a

            _vb = sVE.v
            _ab = sAC.v
            for j, i in enumerate(tiles):
                M[f"pp_evac_t{i}_ve"] = _vb + j + 1
                M[f"pp_evac_t{i}_ac"] = _ab + j + 1

            @block.tensor
            def _(tensor):
                if pb0_ve_wait:
                    tensor.wait_ge(sVE.h, pb0_ve_wait)
                if pb0_ac_wait:
                    tensor.wait_ge(sAC.h, pb0_ac_wait)
                tensor.wait_ge(sIN.h, M["in_done"])
                if g > 0:
                    tensor.wait_ge(sVE.h, M[f"pp{g-1}_evac_ve"])
                    tensor.wait_ge(sAC.h, M[f"pp{g-1}_evac_ac"])
                for j, i in enumerate(tiles):
                    tensor.wait_ge(sAC.h, M[f"pp_apply_t{i}_ac"])
                    if j > 0:
                        tensor.wait_ge(sVE.h, M[f"pp_evac_t{tiles[j-1]}_ve"])
                    for q in range(4):
                        mm = tensor.matmul(
                            pf[0][:, 128 * q:128 * (q + 1)],
                            XH[:, 1024 * j + 128 * q:1024 * j + 128 * (q + 1)],
                            IDE[:], start=True, stop=True)
                    mm.then_inc(sPE.h, 1)
                    sPE.v += 1
                    M[f"pp_tpA_t{i}_pe"] = sPE.v
                    if j > 0:
                        tensor.wait_ge(sAC.h, M[f"pp_evac_t{tiles[j-1]}_ac"])
                    for q in range(4):
                        mm = tensor.matmul(
                            pf[1][:, 128 * q:128 * (q + 1)],
                            XH[:, 1024 * j + 128 * (q + 4):1024 * j + 128 * (q + 5)],
                            IDE[:], start=True, stop=True)
                    M[f"pp_tp_t{i}_pe"] = sPE.tick(mm)
                M[f"pp{g}_tp_pe"] = sPE.v

            @block.vector
            def _(vector):
                for j, i in enumerate(tiles):
                    vector.wait_ge(sPE.h, M[f"pp_tpA_t{i}_pe"])
                    vector.tensor_copy(XHT[:, 1024 * i:1024 * i + 512], pf[0][:])
                    v = sVE.dtick(vector)
                    assert v == M[f"pp_evac_t{i}_ve"]
                M[f"pp{g}_evac_ve"] = sVE.v

            @block.scalar
            def _(scalar):
                for j, i in enumerate(tiles):
                    scalar.wait_ge(sPE.h, M[f"pp_tp_t{i}_pe"])
                    scalar.activation(
                        out=XHT[:, 1024 * i + 512:1024 * (i + 1)], in_=pf[1][:],
                        func=AF.Identity)
                    a = sAC.dtick(scalar)
                    assert a == M[f"pp_evac_t{i}_ac"]
                M[f"pp{g}_evac_ac"] = sAC.v

            if not is_ffn:
                return

            @block.tensor
            def _(tensor):
                tensor.wait_ge(sVE.h, M[f"pp{g}_evac_ve"])
                tensor.wait_ge(sAC.h, M[f"pp{g}_evac_ac"])
                tensor.wait_ge(sWS.h, WS_ALL)
                tensor.wait_ge(sVE.h, M.get("pf4_free_ve", M["setup_ve"]))
                tensor.wait_ge(sVE.h, M["setup_ve"])
                if M.get("mpp_busy_ve") is not None:
                    tensor.wait_ge(sVE.h, M["mpp_busy_ve"])
                if M.get(f"last_router_meq_ve_p{par}") is not None:
                    tensor.wait_ge(sVE.h, M[f"last_router_meq_ve_p{par}"])
                    tensor.wait_ge(sAC.h, M[f"last_router_expo_ac_p{par}"])
                if M.get("last_router_pe") is not None:
                    tensor.wait_ge(sVE.h, M.get("last_router_meq_ve_any", 0))
                    tensor.wait_ge(sAC.h, M.get("last_router_expo_ac_any", 0))
                for j, i in enumerate(tiles):
                    for dc in range(NDC):
                        tensor.matmul(pf[4][:, 8 * j:8 * (j + 1)], xht_lhsT(i, dc),
                                      RWT[:, 8 * dc:8 * (dc + 1)],
                                      start=(dc == 0), stop=False)
                    mm = tensor.matmul(pf[4][:, 8 * j:8 * (j + 1)], ONES[0:1, 0:128],
                                       RB[0:1, :], start=False, stop=True)
                M[f"pp{g}_router_pe"] = sPE.tick(mm)
                M["last_router_pe"] = sPE.v

            @block.scalar
            def _(scalar):
                scalar.wait_ge(sPE.h, M[f"pp{g}_router_pe"])
                if M.get("last_router_mpp_ve") is not None:
                    scalar.wait_ge(sVE.h, M["last_router_mpp_ve"])
                scalar.activation(out=EXPO[:], in_=pf[4][:, 0:32], func=AF.Exp)
                a = sAC.dtick(scalar)
                M[f"pp{g}_expo_ac"] = a
                M[f"last_router_expo_ac_p{par}"] = a
                M["last_router_expo_ac_any"] = a

            @block.vector
            def _(vector):
                vector.wait_ge(sPE.h, M[f"pp{g}_router_pe"])
                v = sVE.tick(vector.tensor_reduce(
                    LMX[:].rearrange("p (a b) -> p a b", b=1),
                    pf[4][:, 0:32].rearrange("p (a b) -> p a b", b=8),
                    axis=AX.X, op=ALU.max))
                vector.wait_ge(sVE.h, v)
                for j in range(4):
                    v = sVE.tick(vector.tensor_scalar(
                        MEQ[:, 8 * j:8 * (j + 1)], pf[4][:, 8 * j:8 * (j + 1)],
                        LMX[:, j:j + 1], None, op0=ALU.is_equal))
                M[f"last_router_meq_ve_p{par}"] = v
                M["last_router_meq_ve_any"] = v
                vector.wait_ge(sVE.h, v)
                for j in range(4):
                    v = sVE.tick(vector.tensor_tensor_scan(
                        CSU[:, 8 * j:8 * (j + 1)], MEQ[:, 8 * j:8 * (j + 1)],
                        ONESF[:], 0.0, op0=ALU.add, op1=ALU.mult))
                vector.wait_ge(sVE.h, v)
                v = sVE.tick(vector.tensor_scalar(SEL[:], CSU[:], 1.0, None, op0=ALU.is_equal))
                vector.wait_ge(sVE.h, v)
                v = sVE.tick(vector.tensor_mul(SEL[:], SEL[:], MEQ[:]))
                vector.wait_ge(sAC.h, M[f"pp{g}_expo_ac"])
                vector.wait_ge(sVE.h, v)
                v = sVE.tick(vector.tensor_mul(T2[:], SEL[:], EXPO[:]))
                vector.wait_ge(sVE.h, v)
                v = sVE.tick(vector.tensor_reduce(
                    ZR[:].rearrange("p (a b) -> p a b", b=1),
                    EXPO[:].rearrange("p (a b) -> p a b", b=8),
                    axis=AX.X, op=ALU.add))
                vector.wait_ge(sVE.h, v)
                v = sVE.tick(vector.reciprocal(ZR[:], ZR[:]))
                vector.wait_ge(sVE.h, v)
                v = sVE.tick(vector.tensor_scalar_mul(RZ2[:], ZR[:], 0.5))
                vector.wait_ge(sVE.h, v)
                for j, i in enumerate(tiles):
                    v = sVE.tick(vector.tensor_scalar_mul(
                        MPP[:, 8 * i:8 * (i + 1)], T2[:, 8 * j:8 * (j + 1)],
                        RZ2[:, j:j + 1]))
                M[f"pp{g}_mpp_ve"] = sVE.v
                M["last_router_mpp_ve"] = sVE.v

        def emit_ffn(site):
            g0 = pp_state["n"]
            for gg in range(4):
                tiles = list(range(4 * gg, 4 * gg + 4))
                stats_wait = None
                pb0_ve = pb0_ac = None
                if site == "ffn2" and gg == 0:
                    pb0_ve = M["sc15_y_ve"]
                    pb0_ac = M["nb3_free_ac"]
                if site == "ffn1" and gg == 0:
                    pb0_ve = M["bb_ve"]
                emit_prepass_group(tiles, site, stats_ve_wait=stats_wait,
                                   pb0_ve_wait=pb0_ve, pb0_ac_wait=pb0_ac)
            xht_ve = M[f"pp{g0+3}_evac_ve"]
            xht_ac = M[f"pp{g0+3}_evac_ac"]
            mpp_ready = M[f"pp{g0+3}_mpp_ve"]

            if trunc == "prepass" and site == "ffn1":
                M[f"{site}_stt_done_ve"] = M[f"pp{g0+3}_mpp_ve"]
                for i in range(NTILES):
                    for h in range(2):
                        M[f"{site}_stt_t{i}_h{h}_ve"] = M[f"pp{g0+3}_mpp_ve"]
                M[f"{site}_pe_done"] = sPE.v
                for e in range(E):
                    for h in range(2):
                        M[f"gate_{site}_e{e}_h{h}"] = sPE.v
                M["mpp_busy_ve"] = M[f"{site}_stt_done_ve"]
                return
            groups = [(e, h, i) for e in range(E) for h in range(2) for i in range(NTILES)]
            base = sVE.v
            for gi in range(len(groups)):
                M[f"{site}_stt_g{gi}_ve"] = base + gi + 1

            @block.tensor
            def _(tensor):
                tensor.wait_ge(sVE.h, xht_ve)
                tensor.wait_ge(sAC.h, xht_ac)
                if site == "ffn1":
                    tensor.wait_ge(sVE.h, M["setup_ve"])
                else:
                    tensor.wait_ge(sVE.h, M["sc15_y_ve"])
                    tensor.wait_ge(sAC.h, M["nb2_free_ac"])
                    tensor.wait_ge(sAC.h, M["nb3_free_ac"])
                for gi, (e, h, i) in enumerate(groups):
                    bank = pf[gi % 3]
                    if gi >= 3:
                        tensor.wait_ge(sVE.h, M[f"{site}_stt_g{gi-3}_ve"])
                    if i == 0:
                        _s1, _v1 = w_after[f"{site}_ew_e{e}_h{h}"]
                        tensor.wait_ge(_s1.h, _v1)
                        _s2, _v2 = w_after[f"{site}_eb_e{e}"]
                        tensor.wait_ge(_s2.h, _v2)
                    buf = WE0 if (2 * e + h) % 2 == 0 else WE1
                    for dc in range(NDC):
                        tensor.matmul(bank[:], xht_lhsT(i, dc),
                                      buf[:, 512 * dc:512 * (dc + 1)],
                                      start=(dc == 0), stop=False)
                    mm = tensor.matmul(bank[:], ONES[0:1, 0:128],
                                       EBR[e % 2][0:1, 512 * h:512 * (h + 1)],
                                       start=False, stop=True)
                    p = sPE.tick(mm)
                    M[f"{site}_mm_g{gi}_pe"] = p
                    if i == NTILES - 1:
                        M[f"gate_{site}_e{e}_h{h}"] = p
                M[f"{site}_pe_done"] = sPE.v

            @block.vector
            def _(vector):
                vector.wait_ge(sVE.h, mpp_ready)
                vector.wait_ge(sX.h, 16)
                if site == "ffn2":
                    vector.wait_ge(sPO.h, M["attn_done_po"])
                for gi, (e, h, i) in enumerate(groups):
                    bank = pf[gi % 3]
                    vector.wait_ge(sPE.h, M[f"{site}_mm_g{gi}_pe"])
                    if gi >= 32:
                        vector.wait_ge(sVE.h, M[f"{site}_stt_g{gi-32}_ve"])
                    if e == 0:
                        vector.wait_ge(sAC.h, M[f"pp_apply_t{i}_ac"])
                    v = sVE.tick(vector.scalar_tensor_tensor(
                        ST[:, tile_cols(i, h)], bank[:], MPP[:, 8 * i + e:8 * i + e + 1],
                        ST[:, tile_cols(i, h)], op0=ALU.mult, op1=ALU.add))
                    assert v == M[f"{site}_stt_g{gi}_ve"]
                    if e == E - 1:
                        M[f"{site}_stt_t{i}_h{h}_ve"] = v
                M[f"{site}_stt_done_ve"] = sVE.dtick(vector)
            M["mpp_busy_ve"] = M[f"{site}_stt_done_ve"]

        def emit_feat_evac(tag, bank, dst, dst_cols, pe_target, prev_ve):
            @block.scalar
            def _(scalar):
                scalar.wait_ge(sPE.h, pe_target)
                if prev_ve:
                    scalar.wait_ge(sVE.h, prev_ve)
                a = sAC.tick(scalar.activation(out=FSC[:], in_=bank[:], func=AF.Relu,
                                               scale=-1.0))
                scalar.wait_ge(sAC.h, a)
                scalar.activation(out=FSC2[:], in_=FSC[:], func=AF.Exp, scale=-1.0)
                M[f"{tag}_ac"] = sAC.dtick(scalar)

            @block.vector
            def _(vector):
                vector.wait_ge(sPE.h, pe_target)
                if prev_ve:
                    vector.wait_ge(sVE.h, prev_ve)
                v = sVE.tick(vector.tensor_scalar_max(FSC3[:], bank[:], 0.0))
                vector.wait_ge(sVE.h, v)
                vector.wait_ge(sAC.h, M[f"{tag}_ac"])
                vector.tensor_add(dst[:, dst_cols], FSC2[:], FSC3[:])
                M[f"{tag}_ve"] = sVE.dtick(vector)

        def emit_scan_chunk(st, c):
            i = 4 * st + c
            cg = i

            @block.tensor
            def _(tensor):
                tensor.wait_ge(sVE.h, M[f"at{st}_kfp_ve"])
                tensor.wait_ge(sVE.h, M[f"at{st}_vos_ve"])
                if cg > 0:
                    tensor.wait_ge(sAC.h, M[f"sc{cg-1}_otev_ac"])
                for fc in range(2):
                    mm = tensor.matmul(pf[7][:, 128 * fc:128 * (fc + 1)],
                                       KFT[:, 512 * fc + 128 * c:512 * fc + 128 * (c + 1)],
                                       IDE[:], start=True, stop=True)
                M[f"sc{cg}_ktokT_pe"] = sPE.tick(mm)

            @block.vector
            def _(vector):
                vector.wait_ge(sPE.h, M[f"sc{cg}_ktokT_pe"])
                vector.tensor_copy(KTOK[:], pf[7][:, 0:256])
                M[f"sc{cg}_ktok_ve"] = sVE.dtick(vector)

            bank_last_mask = {}
            for kv in range(4):
                heads = list(range(4 * kv, 4 * kv + 4))
                nb_i = 2 + (kv % 2)
                nbank = pf[nb_i]

                @block.tensor
                def _(tensor):
                    if kv == 0:
                        tensor.wait_ge(sVE.h, M[f"at{st}_qev_fc7_ve"])
                        if cg > 0:
                            tensor.wait_ge(sVE.h, M[f"sc{cg-1}_y_ve"])
                    for jj, hq in enumerate(heads):
                        sb_i = jj % 2
                        sbank = pf[sb_i]
                        fcq, _bp = q_slot(hq)
                        if bank_last_mask.get(sb_i) is not None:
                            tensor.wait_ge(sVE.h, bank_last_mask[sb_i])
                        mm = tensor.matmul(
                            sbank[:, 0:128],
                            KFP[:, 512 * kv + 128 * c:512 * kv + 128 * (c + 1)],
                            QFT[:, 512 * fcq + 128 * c:512 * fcq + 128 * (c + 1)],
                            start=True, stop=True)
                        M[f"sc{cg}_kv{kv}_s{jj}_pe"] = sPE.tick(mm)

                _mb = sVE.v
                for jj in range(4):
                    M[f"sc{cg}_kv{kv}_sm{jj}_ve"] = _mb + jj + 1

                @block.vector
                def _(vector):
                    if M.get(f"smslots_busy_kv{kv}_pe") is not None:
                        vector.wait_ge(sPE.h, M[f"smslots_busy_kv{kv}_pe"])
                    for jj, hq in enumerate(heads):
                        vector.wait_ge(sPE.h, M[f"sc{cg}_kv{kv}_s{jj}_pe"])
                        v = sVE.tick(vector.tensor_tensor(
                            SMB[:, 128 * jj:128 * (jj + 1)], pf[jj % 2][:, 0:128], TRIU[:],
                            op=ALU.mult))
                        assert v == M[f"sc{cg}_kv{kv}_sm{jj}_ve"]
                        bank_last_mask[jj % 2] = v
                    M[f"sc{cg}_kv{kv}_smD_ve"] = sVE.dtick(vector)

                @block.tensor
                def _(tensor):
                    if M.get(f"nb{nb_i}_free_ac") is not None:
                        tensor.wait_ge(sAC.h, M[f"nb{nb_i}_free_ac"])
                    tensor.wait_ge(sVE.h, M[f"at{st}_vos_ve"])
                    if cg > 0:
                        tensor.wait_ge(sVE.h, M[f"sc{cg-1}_mzb_ve"])
                    tensor.wait_ge(sVE.h, M[f"sc{cg}_kv{kv}_smD_ve"])
                    for jj, hq in enumerate(heads):
                        fcq, _bp = q_slot(hq)
                        tensor.matmul(nbank[:, 65 * jj:65 * jj + 65],
                                      SMB[:, 128 * jj:128 * (jj + 1)],
                                      VOS[:, 260 * c + 65 * kv:260 * c + 65 * (kv + 1)],
                                      start=True, stop=False)
                        mm = tensor.matmul(nbank[:, 65 * jj:65 * jj + 65],
                                           QFT[:, 512 * fcq + 128 * c:512 * fcq + 128 * (c + 1)],
                                           MZB[:, 65 * kv:65 * (kv + 1)],
                                           start=False, stop=True)
                    M[f"sc{cg}_kv{kv}_num_pe"] = sPE.tick(mm)
                    M[f"smslots_busy_kv{(kv + 1) % 4}_pe"] = sPE.v

                @block.vector
                def _(vector):
                    vector.wait_ge(sPE.h, M[f"sc{cg}_kv{kv}_num_pe"])
                    v = sVE.tick(vector.tensor_scalar_add(
                        DEN[:, 4 * kv:4 * kv + 4].rearrange("p (a b) -> p a b", b=1),
                        nbank[:, 0:260].rearrange("p (a b) -> p a b", b=65)[:, :, 64:65],
                        DEN_EPS))
                    vector.wait_ge(sVE.h, v)
                    vector.reciprocal(RCP[:, 4 * kv:4 * kv + 4],
                                      DEN[:, 4 * kv:4 * kv + 4])
                    M[f"sc{cg}_kv{kv}_rcp_ve"] = sVE.dtick(vector)

                @block.scalar
                def _(scalar):
                    scalar.wait_ge(sVE.h, M[f"sc{cg}_kv{kv}_rcp_ve"])
                    if cg > 0:
                        scalar.wait_ge(sPE.h, M[f"sc{cg-1}_oT_pe"])
                    for jj, hq in enumerate(heads):
                        scalar.activation(
                            out=OSB[:, 64 * hq:64 * (hq + 1)],
                            in_=nbank[:, 65 * jj:65 * jj + 64],
                            func=AF.Identity, scale=RCP[:, 4 * kv + jj:4 * kv + jj + 1])
                        if jj < 3:
                            sAC.tick(scalar.nop())
                    a = sAC.dtick(scalar)
                    M[f"nb{nb_i}_free_ac"] = a
                    M[f"sc{cg}_kv{kv}_div_ac"] = a

            @block.tensor
            def _(tensor):
                tensor.wait_ge(sVE.h, M[f"sc{cg}_ktok_ve"])
                if cg > 0:
                    tensor.wait_ge(sVE.h, M[f"sc{cg-1}_mzf_ve"])
                else:
                    tensor.wait_ge(sVE.h, M["setup_ve"])
                for kv in range(4):
                    par = kv % 2
                    bank = pf[5 + par]
                    colo = 65 * (kv // 2)
                    mm = tensor.matmul(bank[64 * par:64 * par + 64, colo:colo + 65],
                                       KTOK[:, 128 * (kv // 2) + 64 * par:
                                            128 * (kv // 2) + 64 * par + 64],
                                       VOS[:, 260 * c + 65 * kv:260 * c + 65 * (kv + 1)],
                                       start=True, stop=True,
                                       tile_position=(0, 64 * par), skip_group_check=True)
                M[f"sc{cg}_state_pe"] = sPE.tick(mm)

            @block.vector
            def _(vector):
                vector.wait_ge(sPE.h, M[f"sc{cg}_state_pe"])
                if cg == 0:
                    vector.wait_ge(sVE.h, M["init_ve"])
                v = 0
                for kv in range(4):
                    par = kv % 2
                    rows = slice(64 * par, 64 * par + 64)
                    colo = 65 * (kv // 2)
                    v = sVE.tick(vector.tensor_tensor(
                        MZF[rows, 65 * kv:65 * (kv + 1)],
                        pf[5 + par][rows, colo:colo + 65],
                        MZF[rows, 65 * kv:65 * (kv + 1)], op=ALU.add))
                vector.wait_ge(sVE.h, v)
                M[f"sc{cg}_mzf_ve"] = v
                for kv in range(4):
                    par = kv % 2
                    rows = slice(64 * par, 64 * par + 64)
                    vector.tensor_copy(MZB[rows, 65 * kv:65 * (kv + 1)],
                                       MZF[rows, 65 * kv:65 * (kv + 1)])
                    if kv < 3:
                        sVE.tick(vector.nop())
                M[f"sc{cg}_mzb_ve"] = sVE.dtick(vector)

            M[f"sc{cg}_oTA_pred_ve"] = sVE.v + 1

            @block.tensor
            def _(tensor):
                for kv in range(4):
                    tensor.wait_ge(sAC.h, M[f"sc{cg}_kv{kv}_div_ac"])
                tensor.wait_ge(sVE.h, M[f"sc{cg}_ktok_ve"])
                for q in range(4):
                    mm = tensor.matmul(pf[7][:, 128 * q:128 * (q + 1)],
                                       OSB[:, 128 * q:128 * (q + 1)], IDE[:],
                                       start=True, stop=True)
                mm.then_inc(sPE.h, 1)
                sPE.v += 1
                M[f"sc{cg}_oTA_pe"] = sPE.v
                tensor.wait_ge(sVE.h, M[f"sc{cg}_oTA_pred_ve"])
                for q in range(4):
                    mm = tensor.matmul(pf[7][:, 128 * q:128 * (q + 1)],
                                       OSB[:, 128 * (q + 4):128 * (q + 5)], IDE[:],
                                       start=True, stop=True)
                M[f"sc{cg}_oT_pe"] = sPE.tick(mm)

            @block.vector
            def _(vector):
                vector.wait_ge(sPE.h, M[f"sc{cg}_oTA_pe"])
                if cg > 0:
                    vector.wait_ge(sPE.h, M[f"sc{cg-1}_yo_h1_pe"])
                vector.tensor_copy(OT[:, 0:512], pf[7][:])
                v = sVE.dtick(vector)
                assert v == M[f"sc{cg}_oTA_pred_ve"]
                M[f"sc{cg}_otev_ve"] = v

            @block.scalar
            def _(scalar):
                scalar.wait_ge(sPE.h, M[f"sc{cg}_oT_pe"])
                if cg > 0:
                    scalar.wait_ge(sPE.h, M[f"sc{cg-1}_yo_h1_pe"])
                scalar.activation(out=OT[:, 512:1024], in_=pf[7][:], func=AF.Identity)
                M[f"sc{cg}_otev_ac"] = sAC.dtick(scalar)

            @block.tensor
            def _(tensor):
                tensor.wait_ge(sVE.h, M[f"sc{cg}_otev_ve"])
                tensor.wait_ge(sAC.h, M[f"sc{cg}_otev_ac"])
                tensor.wait_ge(sWS.h, WS_ALL)
                for h in range(2):
                    bank = pf[h]
                    for oc in range(NDC):
                        tensor.matmul(bank[:], OT[:, 128 * oc:128 * (oc + 1)],
                                      WO[:, 1024 * oc + 512 * h:1024 * oc + 512 * (h + 1)],
                                      start=(oc == 0), stop=False)
                    mm = tensor.matmul(bank[:], ONES[0:1, 0:128],
                                       BO[0:1, 512 * h:512 * (h + 1)],
                                       start=False, stop=True)
                    M[f"sc{cg}_yo_h{h}_pe"] = sPE.tick(mm)

            @block.vector
            def _(vector):
                v = 0
                vector.wait_ge(sAC.h, M[f"pp_apply_t{i}_ac"])
                for h in range(2):
                    vector.wait_ge(sPE.h, M[f"sc{cg}_yo_h{h}_pe"])
                    if cg > 0 and h == 0:
                        vector.wait_ge(sAC.h, M[f"sc{cg-1}_z_ac"])
                    v = sVE.tick(vector.tensor_add(
                        YSB[:, 512 * h:512 * (h + 1)], pf[h][:],
                        XH[:, 1024 * c + 512 * h:1024 * c + 512 * (h + 1)]))
                M[f"sc{cg}_y_ve"] = v
                vector.wait_ge(sVE.h, v)
                vector.bn_stats(YST[:, 0:6], YSB[:, 0:512])
                v = sVE.tick(vector.bn_stats(YST[:, 6:12], YSB[:, 512:1024]))
                vector.wait_ge(sVE.h, v)
                v = sVE.tick(vector.bn_aggr(YMV[:], YST[:]))
                vector.wait_ge(sVE.h, v)
                vector.tensor_scalar_add(YRS[:], YMV[:, 1:2], LN_EPS)
                M[f"sc{cg}_yrs_ve"] = sVE.dtick(vector)

            @block.scalar
            def _(scalar):
                scalar.wait_ge(sVE.h, M[f"sc{cg}_yrs_ve"])
                a = sAC.tick(scalar.activation(out=YRS[:], in_=YRS[:], func=AF.Ln))
                scalar.wait_ge(sAC.h, a)
                scalar.activation(out=YRS[:], in_=YRS[:], func=AF.Exp, scale=-0.5)
                M[f"sc{cg}_ysqrt_ac"] = sAC.dtick(scalar)

            @block.vector
            def _(vector):
                vector.wait_ge(sAC.h, M[f"sc{cg}_ysqrt_ac"])
                vector.scalar_tensor_tensor(YNM[:], YMV[:, 0:1], -1.0, YRS[:],
                                            op0=ALU.mult, op1=ALU.mult)
                M[f"sc{cg}_ynm_ve"] = sVE.dtick(vector)

            @block.scalar
            def _(scalar):
                scalar.wait_ge(sVE.h, M[f"sc{cg}_ynm_ve"])
                if cg > 0:
                    scalar.wait_ge(sPO.h, M[f"sc{cg-1}_tsb1_po"])
                scalar.activation(out=ZSB[:], in_=YSB[:], func=AF.Identity,
                                  bias=YNM[:], scale=YRS[:])
                M[f"sc{cg}_z_ac"] = sAC.dtick(scalar)

            @block.gpsimd
            def _(gpsimd):
                gpsimd.wait_ge(sAC.h, M[f"sc{cg}_z_ac"])
                gpsimd.wait_ge(sVE.h, M["ffn1_stt_done_ve"])
                p = sPO.tick(gpsimd.tensor_tensor(TSB[:], ZSB[:], GB[:], op=ALU.mult))
                M[f"sc{cg}_tsb1_po"] = p
                gpsimd.wait_ge(sPO.h, p)
                p = sPO.tick(gpsimd.tensor_tensor(TSB[:], TSB[:], BB[:], op=ALU.add))
                gpsimd.wait_ge(sPO.h, p)
                p = sPO.tick(gpsimd.tensor_tensor(ST[:, tile_cols(i)], ST[:, tile_cols(i)],
                                                  TSB[:], op=ALU.add))
                M[f"attn_st_t{i}_po"] = p

        def emit_attn():
            for st in range(4):
                tiles = list(range(4 * st, 4 * st + 4))
                g = pp_state["n"]
                stats_wait = max(M[f"ffn1_stt_t{i}_h{h}_ve"] for i in tiles for h in range(2))
                apply_wait = M[f"sc{4*st-1}_y_ve"] if st > 0 else None
                pb0_ve = M[f"sc{4*st-1}_y_ve"] if st > 0 else M["ffn1_stt_done_ve"]
                pb0_ac = M[f"sc{4*st-1}_otev_ac"] if st > 0 else None
                emit_prepass_group(tiles, "attn", stats_ve_wait=stats_wait,
                                   pb0_ve_wait=pb0_ve, pb0_ac_wait=pb0_ac,
                                   apply_ve_wait=apply_wait)
                xht_ve = M[f"pp{g}_evac_ve"]
                xht_ac = M[f"pp{g}_evac_ac"]

                _vb = sVE.v
                for fc in range(NDC):
                    M[f"at{st}_qfp{fc}_ve"] = _vb + 2 * (fc + 1)

                @block.tensor
                def _(tensor):
                    tensor.wait_ge(sVE.h, xht_ve)
                    tensor.wait_ge(sAC.h, xht_ac)
                    tensor.wait_ge(sWS.h, WS_ALL)
                    if st == 0:
                        tensor.wait_ge(sVE.h, M["setup_ve"])
                        tensor.wait_ge(sVE.h, M["init_ve"])
                        tensor.wait_ge(sIN.h, M["in_done"])
                    else:
                        tensor.wait_ge(sVE.h, M[f"sc{4*st-1}_y_ve"])
                    for fc in range(NDC):
                        bank = pf[fc % 2]
                        if fc >= 2:
                            tensor.wait_ge(sVE.h, M[f"at{st}_qfp{fc-2}_ve"])
                        for dc in range(NDC):
                            tensor.matmul(bank[:],
                                          WQ[:, 1024 * dc + 128 * fc:1024 * dc + 128 * (fc + 1)],
                                          xht_mov(st, dc), start=(dc == 0), stop=False)
                        mm = tensor.matmul(bank[:], BQ[0:1, 128 * fc:128 * (fc + 1)],
                                           ONES[0:1, 0:512], start=False, stop=True)
                        M[f"at{st}_q_fc{fc}_pe"] = sPE.tick(mm)

                prev = M[f"at{st-1}_kev_fc1_ve"] if st > 0 else None
                for fc in range(NDC):
                    emit_feat_evac(f"at{st}_qev_fc{fc}", pf[fc % 2], QFT,
                                   slice(512 * fc, 512 * (fc + 1)),
                                   M[f"at{st}_q_fc{fc}_pe"], prev)
                    prev = M[f"at{st}_qev_fc{fc}_ve"]
                    assert prev == M[f"at{st}_qfp{fc}_ve"]
                M[f"at{st}_qev_fc7_ve"] = prev

                @block.tensor
                def _(tensor):
                    if st > 0:
                        tensor.wait_ge(sAC.h, M["nb2_free_ac"])
                        tensor.wait_ge(sAC.h, M["nb3_free_ac"])
                    for fc in range(2):
                        bank = pf[2 + fc]
                        for dc in range(NDC):
                            tensor.matmul(bank[:],
                                          WKE[:, 256 * dc + 128 * fc:256 * dc + 128 * (fc + 1)],
                                          xht_mov(st, dc), start=(dc == 0), stop=False)
                        mm = tensor.matmul(bank[:], BKE[0:1, 128 * fc:128 * (fc + 1)],
                                           ONES[0:1, 0:512], start=False, stop=True)
                        M[f"at{st}_k_fc{fc}_pe"] = sPE.tick(mm)

                prevk = prev
                for fc in range(2):
                    emit_feat_evac(f"at{st}_kev_fc{fc}", pf[2 + fc], KFT,
                                   slice(512 * fc, 512 * (fc + 1)),
                                   M[f"at{st}_k_fc{fc}_pe"], prevk)
                    prevk = M[f"at{st}_kev_fc{fc}_ve"]

                @block.vector
                def _(vector):
                    if st > 0:
                        vector.wait_ge(sPE.h, M[f"sc{4*st-1}_kv3_num_pe"])
                    for kv in range(4):
                        b = 64 * (kv % 2)
                        fc = kv // 2
                        vector.wait_ge(sVE.h, M[f"at{st}_kev_fc{fc}_ve"])
                        vector.tensor_copy(
                            KFP[b:b + 64, 512 * kv:512 * (kv + 1)],
                            KFT[b:b + 64, 512 * fc:512 * (fc + 1)])
                        v = sVE.tick(vector.engine_nop()) if False else sVE.dtick(vector)
                    M[f"at{st}_kfp_ve"] = v

                for cc in range(4):
                    M[f"at{st}_vev_pred_c{cc}"] = sVE.v + cc + 1

                @block.tensor
                def _(tensor):
                    if st == 0:
                        tensor.wait_ge(sVE.h, M["ffn1_rtr_free_ve"])
                    for cc in range(4):
                        ii = 4 * st + cc
                        if st > 0 or cc > 0:
                            tensor.wait_ge(sVE.h, M["vev_prev_ve"])
                        if st > 0 and cc == 0:
                            tensor.wait_ge(sPE.h, M[f"sc{4*st-1}_state_pe"])
                            tensor.wait_ge(sPE.h, M[f"sc{4*st-1}_kv3_num_pe"])
                        for dc in range(NDC):
                            tensor.matmul(pf[4][:, 0:256], xht_lhsT(ii, dc),
                                          WVE[:, 256 * dc:256 * (dc + 1)],
                                          start=(dc == 0), stop=False)
                        mm = tensor.matmul(pf[4][:, 0:256], ONES[0:1, 0:128],
                                           BVE[0:1, 0:256], start=False, stop=True)
                        M[f"at{st}_v_c{cc}_pe"] = sPE.tick(mm)
                        M["vev_prev_ve"] = M[f"at{st}_vev_pred_c{cc}"]

                @block.vector
                def _(vector):
                    if st == 0:
                        vector.wait_ge(sVE.h, M["init_ve"])
                    for cc in range(4):
                        vector.wait_ge(sPE.h, M[f"at{st}_v_c{cc}_pe"])
                        dst = VOS[:, 260 * cc:260 * (cc + 1)].rearrange(
                            "p (k o) -> p k o", k=4)[:, :, 0:64]
                        src = pf[4][:, 0:256].rearrange("p (k o) -> p k o", k=4)
                        v = sVE.tick(vector.tensor_copy(dst, src))
                        assert v == M[f"at{st}_vev_pred_c{cc}"]
                    M[f"at{st}_vos_ve"] = sVE.dtick(vector)

                for cc in range(4):
                    emit_scan_chunk(st, cc)

            M["attn_last_otev_ve"] = M["sc15_otev_ve"]
            M["attn_last_otev_ac"] = M["sc15_otev_ac"]

            @block.gpsimd
            def _(gpsimd):
                sPO.dtick(gpsimd)
            M["attn_done_po"] = sPO.v
            M["pf4_free_ve"] = M["at3_vos_ve"]

        if trunc == "setup":
            emit_wdma(part_a[:15])
            @block.gpsimd
            def _(gpsimd):
                gpsimd.wait_ge(sVE.h, M["setup_ve"])
                gpsimd.dma_start(
                    out=out_h.rearrange("(i p) f -> p i f", p=128),
                    in_=ST[:].rearrange("p (i f) -> p i f", i=NTILES),
                ).then_inc(sOUT.h, 16)
                gpsimd.wait_ge(sOUT.h, 16)
            return nc
        if "ffn1" in stages:
            emit_ffn("ffn1")
            M["ffn1_rtr_free_ve"] = M["ffn1_stt_done_ve"]

        emit_wdma(part_a)

        if "attn" in stages:
            emit_attn()

        if "ffn2" in stages:
            @block.vector
            def _(vector):
                vector.wait_ge(sPO.h, M["attn_done_po"])

            @block.scalar
            def _(scalar):
                scalar.wait_ge(sPO.h, M["attn_done_po"])
            emit_ffn("ffn2")

        if "ffn2" in stages:
            emit_wdma(part_b)

        last = "ffn2" if "ffn2" in stages else ("attn" if "attn" in stages else "ffn1")

        @block.gpsimd
        def _(gpsimd):
            if last == "ffn2":
                gpsimd.wait_ge(sVE.h, M["ffn2_stt_done_ve"])
            elif last == "attn":
                gpsimd.wait_ge(sPO.h, M["attn_done_po"])
            else:
                gpsimd.wait_ge(sVE.h, M["ffn1_stt_done_ve"])
            gpsimd.dma_start(
                out=out_h.rearrange("(i p) f -> p i f", p=128),
                in_=ST[:].rearrange("p (i f) -> p i f", i=NTILES),
            ).then_inc(sOUT.h, 16)
            sOUT.v += 16
            gpsimd.wait_ge(sOUT.h, sOUT.v)

    return nc


def prep_inputs(d):
    f32 = np.float32
    x = np.ascontiguousarray(np.asarray(d["L0_attn_x"], f32))
    wq = np.ascontiguousarray(np.asarray(d["L0_attn_Wq"], f32)[:, QPERM])
    bq = np.ascontiguousarray(np.asarray(d["L0_attn_bq"], f32)[QPERM][None, :])
    wkT = np.ascontiguousarray(np.asarray(d["L0_attn_Wk"], f32).T)
    wvT = np.ascontiguousarray(np.asarray(d["L0_attn_Wv"], f32).T)
    wcT = np.ascontiguousarray(np.asarray(d["L0_attn_Wc"], f32).T)
    wee = np.ascontiguousarray(np.asarray(d["L0_attn_We"], f32))
    bkc = np.ascontiguousarray(np.asarray(d["L0_attn_bk"], f32).reshape(2, 128).T)
    bvc = np.ascontiguousarray(np.asarray(d["L0_attn_bv"], f32).reshape(2, 128).T)
    wo = np.ascontiguousarray(np.asarray(d["L0_attn_Wo"], f32))
    bo = np.ascontiguousarray(np.asarray(d["L0_attn_bo"], f32)[None, :])
    gam = np.ascontiguousarray(np.asarray(d["L0_attn_gamma"], f32)[None, :])
    bet = np.ascontiguousarray(np.asarray(d["L0_attn_beta"], f32)[None, :])
    rwT = np.ascontiguousarray(np.asarray(d["L0_ffn_router_w"], f32).T)
    rb = np.ascontiguousarray(np.asarray(d["L0_ffn_router_b"], f32)[None, :])
    ew = np.ascontiguousarray(np.asarray(d["L0_ffn_expert_w"], f32).transpose(0, 2, 1))
    eb = np.ascontiguousarray(np.asarray(d["L0_ffn_expert_b"], f32))
    c_ident = np.eye(128, dtype=f32).astype(NBF)
    c_triu = np.triu(np.ones((128, 128), f32)).astype(NBF)
    c_ones = np.ones((1, 512), f32).astype(NBF)
    shared = dict(wq=wq, bq=bq, wkT=wkT, wvT=wvT, wcT=wcT, wee=wee, bkc=bkc,
                  bvc=bvc, wo=wo, bo=bo, gam=gam, bet=bet, rwT=rwT, rb=rb,
                  ew=ew, eb=eb, c_ident=c_ident, c_triu=c_triu, c_ones=c_ones)
    in_maps = []
    for b in range(8):
        m = dict(shared)
        m["x"] = np.ascontiguousarray(x[b])
        in_maps.append(m)
    return in_maps


def kernel(**inputs):
    in_maps = prep_inputs(inputs)
    nc = build_nc()
    res = run_bass_kernel_spmd(nc, in_maps, core_ids=list(range(8)))
    out = np.stack([res.results[b]["out"] for b in range(8)], axis=0)
    return out.astype(np.float32)
